# version 30
# baseline (speedup 1.0000x reference)
import numpy as np
import ml_dtypes
import concourse.bass as bass
import concourse.mybir as mybir
from concourse.bass_utils import run_bass_kernel_spmd

F32 = mybir.dt.float32
BF16 = mybir.dt.bfloat16
U8 = mybir.dt.uint8
AF = mybir.ActivationFunctionType
OP = mybir.AluOpType
AX = mybir.AxisListType.X

MAGIC = 12582912.0
RRLIM = 4194304.0
NEGBIG = -3.0e38
D = 1024
ND = 8
R = 4096
FF = 4096
NF = 32
H = 4
HD = 256
C = 64
TG = 512
NTG = 8
EPSQ = 1e-8
EPSLN = 1e-5

DUMP = ()

SLAB_MAP = {
    "CG0": (0, [128, 1024], F32),
    "CG1": (4096, [128, 1024], F32),
    "WT0": (0, [128, 4096], F32),
    "WT1": (16384, [128, 4096], F32),
    "IBC": (32768, [128, 4096], F32),
    "TQAp": (49152, [128, 4096], F32),
    "TQBp": (65536, [128, 4096], F32),
    "WCST": (49152, [128, 4096], BF16),
    "W2T1": (81920, [128, 1024], BF16),
    "W2T2": (83968, [128, 1024], BF16),
    "XA0": (0, [128, 4096], F32),
    "XA1": (16384, [128, 4096], F32),
    "LNT0": (32768, [128, 4096], F32),
    "XSQ": (49152, [128, 4096], BF16),
    "XBF": (57344, [128, 4096], BF16),
    "RB0": (65536, [128, 512], F32),
    "MRB0": (67584, [128, 512], F32),
    "RB1": (69632, [128, 512], F32),
    "MRB1": (71680, [128, 512], F32),
    "MB0": (73728, [128, 512], F32),
    "MB1": (75776, [128, 512], F32),
    "VB0": (77824, [128, 512], F32),
    "VB1": (79872, [128, 512], F32),
    "TB0": (81920, [128, 512], F32),
    "TB1": (83968, [128, 512], F32),
    "SH0": (86016, [128, 512], F32),
    "RN0": (88064, [128, 512], F32),
    "RN1": (90112, [128, 512], F32),
    "RN2": (92160, [128, 512], F32),
    "RN3": (94208, [128, 512], F32),
    "RNALL": (88064, [128, 2048], F32),
    "GSC": (104448, [128, 2048], F32),
    "UBT": (96256, [128, 4096], BF16),
    "MA0": (0, [128, 4096], F32),
    "MA1": (16384, [128, 4096], F32),
    "MTQ": (32768, [128, 4096], F32),
    "QX0": (49152, [128, 4096], BF16),
    "QX1": (57344, [128, 4096], BF16),
    "HT0": (65536, [128, 512], F32),
    "HT1": (67584, [128, 512], F32),
    "HT2": (69632, [128, 512], F32),
    "HT3": (71680, [128, 512], F32),
    "HTQ": (0, [128, 2048], F32),
    "ATQ": (8192, [128, 2048], F32),
    "XR0": (16384, [128, 2048], F32),
    "XR1": (24576, [128, 2048], F32),
    "XO0": (32768, [128, 2048], F32),
    "XO1": (40960, [128, 2048], F32),
    "YT0": (49152, [128, 256], F32),
    "YT1": (50176, [128, 256], F32),
    "YT2": (51200, [128, 256], F32),
    "YT3": (52224, [128, 256], F32),
    "HTQ4": (86016, [128, 4096], F32),
    "QH0": (53248, [128, 8192], BF16),
    "QH1": (69632, [128, 8192], BF16),
    "AXA": (0, [128, 4096], F32),
    "AXB": (16384, [128, 4096], F32),
    "WKPMB": (32768, [128, 4096], BF16),
    "ATT": (49152, [128, 512], F32),
    "VPM": (51200, [128, 4096], BF16),
    "WKPM": (59392, [128, 4096], BF16),
    "WKPMG": (67584, [128, 4096], BF16),
    "WKPMG2": (75776, [128, 4096], BF16),
    "PSB": (83968, [128, 1024], BF16),
    "PTSB": (86016, [128, 1024], BF16),
    "INTRAT": (88064, [128, 1024], BF16),
    "XSB": (90112, [128, 1024], BF16),
    "QSB": (92160, [128, 1024], BF16),
    "QTSB": (94208, [128, 1024], BF16),
    "ATB": (96256, [128, 1024], BF16),
    "WKC": (98304, [128, 512], BF16),
    "VNB": (99328, [128, 1024], BF16),
    "SBFT": (101376, [128, 2048], BF16),
    "OSB": (105472, [128, 4096], BF16),
}
SLAB_BYTES = 113664


def ap4(tensor, off):
    return bass.AP(tensor, off, [[32, 4], [1, 512]])


class Sem:
    def __init__(self, nc, name):
        self._cm = nc.semaphore(name)
        self.h = self._cm.__enter__()
        self.n = 0


class Builder:
    def __init__(self, nc, dump=(), extern=()):
        self.nc = nc
        self.dump = dump
        self.extern = extern
        self.t = {}

    def dram(self, name, shape, dtype=F32, kind="Internal"):
        if kind == "Internal" and name in self.dump:
            kind = "ExternalOutput"
        if kind == "Internal" and name in self.extern:
            kind = "ExternalInput"
        h = self.nc.dram_tensor(name, shape, dtype, kind=kind)
        self.t[name] = h
        return h

    def sb(self, name, shape, dtype):
        cm = self.nc.sbuf_tensor(name, shape, dtype)
        h = cm.__enter__()
        self.t[name] = h
        return h

    def derive_q(self, e, swname, ntile, ssw, bsrc, bq, with_half):
        t = self.t
        self.fv(e, e.tensor_scalar_mul(t[ssw][:, 0:ntile], t[swname][:, 0:ntile],
                                       t["S1C"][:, 0:1]))
        self.fv(e, e.reciprocal(t["QT5"][:, 0:ntile], t[ssw][:, 0:ntile]))
        self.fv(e, e.tensor_tensor(t["QT6"][:, 0:ntile], t[bsrc][:, 0:ntile],
                                   t["QT5"][:, 0:ntile], OP.mult))
        self.fv(e, e.tensor_scalar_add(t["QT7"][:, 0:ntile], t["QT6"][:, 0:ntile],
                                       MAGIC))
        self.fv(e, e.tensor_scalar_add(t["QT7"][:, 0:ntile], t["QT7"][:, 0:ntile],
                                       -MAGIC))
        self.fv(e, e.tensor_scalar_mul(t["QT8"][:, 0:ntile], t["QT6"][:, 0:ntile],
                                       -1.0))
        self.fv(e, e.tensor_max(t["QT8"][:, 0:ntile], t["QT8"][:, 0:ntile],
                                t["QT6"][:, 0:ntile]))
        self.fv(e, e.tensor_single_scalar(t["QTM"][:, 0:ntile],
                                          t["QT8"][:, 0:ntile], RRLIM, OP.is_lt))
        self.fv(e, e.select(t["QT6"][:, 0:ntile], t["QTM"][:, 0:ntile],
                            t["QT7"][:, 0:ntile], t["QT6"][:, 0:ntile]))
        self.fv(e, e.tensor_tensor(t[bq][:, 0:ntile], t["QT6"][:, 0:ntile],
                                   t[ssw][:, 0:ntile], OP.mult))
        if with_half:
            self.fv(e, e.tensor_scalar_mul(t[ssw][:, 0:ntile], t[ssw][:, 0:ntile],
                                           0.5))
            self.fv(e, e.tensor_scalar_mul(t[bq][:, 0:ntile], t[bq][:, 0:ntile],
                                           0.5))

    def entry_waits(self, e, b):
        for nm, s_ in (("gqS", self.gqS), ("dln", self.dln), ("drn", self.drn),
                       ("dstp0", self.dstp[0]), ("dstp1", self.dstp[1]),
                       ("hts0", self.hts[0]), ("hts1", self.hts[1]),
                       ("hts2", self.hts[2]), ("hts3", self.hts[3])):
            if b[nm] > 0:
                e.wait_ge(s_.h, b[nm])

    def fv(self, e, instr):
        self.ssV.n += 1
        instr.then_inc(self.ssV.h, 1)
        e.wait_ge(self.ssV.h, self.ssV.n)

    def fa(self, e, instr):
        self.ssA.n += 1
        instr.then_inc(self.ssA.h, 1)
        e.wait_ge(self.ssA.h, self.ssA.n)

    def rk_sl(self, c):
        return self.wbuf[c][:, 2048:2560]

    def wk_sl(self, c):
        return self.wbuf[c][:, 2560:3072]

    def ub_sl(self, c):
        return self.wbuf[c][:, 3072:3584]

    def ga_sl(self, c):
        return self.wbuf[c][:, 3584:4096]

    def declare(self):
        d = self.dram
        d("x0f", [D, R], kind="ExternalInput")
        d("w1", [FF, D], kind="ExternalInput")
        d("w2", [D, FF], kind="ExternalInput")
        d("wwt", [D, D], kind="ExternalInput")
        d("wrt", [D, D], kind="ExternalInput")
        d("b1r", [128, 32], kind="ExternalInput")
        d("b2r", [128, 8], kind="ExternalInput")
        d("decay", [1, H], kind="ExternalInput")
        d("logalpha", [1, H], kind="ExternalInput")
        d("cdiffsg", [128, 1024], kind="ExternalInput")
        d("cnstrictg", [128, 1024], kind="ExternalInput")
        d("cdiffstg", [128, 1024], kind="ExternalInput")
        d("cnstricttg", [128, 1024], kind="ExternalInput")
        d("cstricttg", [128, 1024], kind="ExternalInput")
        d("ceyeb", [128, 1024], kind="ExternalInput")
        d("ceye128", [128, 64], kind="ExternalInput")
        d("ceyef", [128, 128], kind="ExternalInput")
        d("ciota", [1, 64], kind="ExternalInput")
        d("criota", [1, 64], kind="ExternalInput")
        d("y", [D, R], kind="ExternalOutput")
        d("x1f", [D, R])
        d("x2f", [D, R])
        d("lnbuf", [D, R])
        d("hbuf", [FF, R])
        d("w1q", [D, FF], BF16)
        d("rkbuf", [D, R], BF16)
        d("ubbuf", [D, R], BF16)
        d("w2q", [FF, D], BF16)
        d("wwtb", [D, D], BF16)
        d("wrtb", [D, D], BF16)
        d("strow", [8, R])
        d("mrow", [1, 128])
        d("misc", [1, 64])
        d("misc2", [1, 64])
        d("gparow", [1, H * 512])
        d("gptrow", [1, 256])
        for i in (1, 2, 3, 4):
            d("cci%d" % i, [1, 64])
            d("cco%d" % i, [1, 64])

        s = self.sb
        self.wbuf = [s("wb%d" % i, [128, 4096], BF16) for i in range(ND)]
        s("ONESB", [128, 1], BF16)
        s("ONES128", [128, 128], BF16)
        s("EYE128", [128, 64], BF16)
        s("EYEF", [128, 128], BF16)
        s("RKTAIL", [128, 8], BF16)
        s("NEGLS", [128, 1024], F32)
        s("NEGLST", [128, 1024], F32)
        s("ALPHALMT", [128, 1024], F32)
        s("EYEB", [128, 1024], F32)
        self.GPAT = [s("GPAT%d" % h, [128, 512], F32) for h in range(H)]
        s("LGB", [128, 4], F32)
        s("ALB", [128, 4], F32)
        s("GC128", [128, 4], F32)
        s("GT128", [128, 4], F32)
        s("S1C", [128, 1], F32)
        s("IS1C", [128, 1], F32)
        s("RMAX", [128, 1], F32)
        s("HMAX", [128, 1], F32)
        s("TMPC", [128, 1], F32)
        s("SWV1", [128, 32], F32)
        s("SWV2", [128, 8], F32)
        s("B1S", [128, 32], F32)
        s("B2S", [128, 8], F32)
        s("S1SW", [128, 32], F32)
        s("B1Q", [128, 32], F32)
        s("S2SW", [128, 8], F32)
        s("B2Q", [128, 8], F32)
        s("QT5", [128, 32], F32)
        s("QT6", [128, 32], F32)
        s("QT7", [128, 32], F32)
        s("QT8", [128, 32], F32)
        s("QTM", [128, 32], U8)
        s("MROWS", [1, 128], F32)
        s("CCS", [1, 64], F32)
        s("SCC", [1, 64], F32)
        s("SCC128", [128, 1], F32)
        s("TMPG", [128, 1], F32)
        s("DRO", [1, 64], F32)
        s("GAMR", [1, 64], F32)
        s("LGR", [1, 64], F32)
        s("ALR", [1, 64], F32)
        s("GCR", [1, 64], F32)
        s("IOTAS", [1, 64], F32)
        s("RIOTAS", [1, 64], F32)
        s("GPAR", [1, 256], F32)
        s("GPTR", [1, 256], F32)

        self.const_vals = [MAGIC, -MAGIC, 1.0 / 127.0, 1.0 / 255.0, 1.0 / D,
                           0.5, EPSLN, EPSQ, 1e-30, 1e-12, -1.0, RRLIM, 64.0,
                           NEGBIG, 1.5, 2.0]
        cb = s("CONSTS", [128, len(self.const_vals)], F32)
        for i, v in enumerate(self.const_vals):
            self.nc.const_aps.register(F32, v, cb[:, i:i + 1])
        s("ZCOL", [128, 1], BF16)
        cbb = s("CONSTB", [128, 4], BF16)
        for i, v in enumerate([1.0, -1.0, 0.0, 0.5]):
            self.nc.const_aps.register(BF16, v, cbb[:, i:i + 1])
        self.const_bf_vals = [1.0, -1.0, 0.0, 0.5]

        slab = s("SLAB", [128, SLAB_BYTES], U8)
        base = self.nc.lookup_mloc(slab).addr
        for nm, (off, shape, dt_) in SLAB_MAP.items():
            self.t[nm] = self.nc.alloc_sbuf_tensor_at(nm, shape, dt_,
                                                      offset=base + off)

        self.PB = []
        self.PBb = []
        for i in range(8):
            cm = self.nc.psum_tensor("PB%d" % i, [128, 512], F32)
            self.PB.append(cm.__enter__())
            self.t["PB%d" % i] = self.PB[i]
            vb = self.nc.virtual_psum_tensor("PBb%d" % i, [128, 1024], BF16)
            self.nc.back_virtual_tensor_with(vb, self.PB[i])
            self.PBb.append(vb)

        for nm in ("sqL", "sqT", "gqS", "gqB", "wqL", "mmG", "veV", "scA",
                   "ccS", "evS", "evV", "stT", "snV", "snS", "snT", "snG",
                   "dln", "drn", "ssV", "ssA"):
            setattr(self, nm, Sem(self.nc, nm))
        self.dsl = [Sem(self.nc, "dsl%d" % i) for i in range(4)]
        self.dsb = [Sem(self.nc, "dsb%d" % i) for i in range(2)]
        self.dstp = [Sem(self.nc, "dstp%d" % i) for i in range(2)]
        self.dtp = [Sem(self.nc, "dtp%d" % i) for i in range(2)]
        self.hts = [Sem(self.nc, "hts%d" % i) for i in range(4)]

    def bases(self):
        d = {nm: getattr(self, nm).n for nm in
             ("sqL", "sqT", "gqS", "gqB", "wqL", "mmG", "veV", "scA",
              "ccS", "evS", "evV", "stT", "snV", "snS", "snT", "snG",
              "dln", "drn")}
        for i, s in enumerate(self.dsl):
            d["dsl%d" % i] = s.n
        for i, s in enumerate(self.dsb):
            d["dsb%d" % i] = s.n
        for i, s in enumerate(self.dstp):
            d["dstp%d" % i] = s.n
        for i, s in enumerate(self.dtp):
            d["dtp%d" % i] = s.n
        for i, s in enumerate(self.hts):
            d["hts%d" % i] = s.n
        return d

    def const_init(self):
        nc, t = self.nc, self.t
        with nc.Block() as blk:
            @blk.vector
            def _(e):
                for i, v in enumerate(self.const_vals):
                    e.memset(t["CONSTS"][:, i:i + 1], v)
                for i, v in enumerate(self.const_bf_vals):
                    e.memset(t["CONSTB"][:, i:i + 1], v)
                e.memset(t["ZCOL"][:, :], 0.0)

    def p0_scalars(self):
        nc, t = self.nc, self.t
        sqL, gqS, gqB, scA = self.sqL, self.gqS, self.gqB, self.scA
        b = self.bases()
        with nc.Block() as blk:
            @blk.sync
            def _(e):
                loads = [
                    (t["B1S"][:, :], t["b1r"].ap()),
                    (t["B2S"][:, :], t["b2r"].ap()),
                    (t["DRO"][0:1, 0:4], t["decay"].ap()),
                    (t["DRO"][0:1, 8:12], t["logalpha"].ap()),
                    (t["IOTAS"][:, :], t["ciota"].ap()),
                    (t["RIOTAS"][:, :], t["criota"].ap()),
                    (t["EYEB"][:, :], t["ceyeb"].ap()),
                ]
                for o, i in loads:
                    e.dma_start(out=o, in_=i).then_inc(sqL.h, 16)
                sqL.n = b["sqL"] + 16 * 7

            @blk.vector
            def _(e):
                e.memset(t["ONESB"][:, :], 1.0)
                e.memset(t["ONES128"][:, :], 1.0)
                e.memset(t["RMAX"][:, :], NEGBIG)
                e.memset(t["HMAX"][:, :], NEGBIG)

            @blk.scalar
            def _(e):
                e.wait_ge(sqL.h, b["sqL"] + 16 * 7)
                self.fa(e, e.activation(t["GAMR"][0:1, 0:4], t["DRO"][0:1, 0:4],
                                        AF.Sigmoid))
                self.fa(e, e.activation(t["LGR"][0:1, 0:4], t["GAMR"][0:1, 0:4],
                                        AF.Ln))
                e.activation(t["ALR"][0:1, 0:4], t["DRO"][0:1, 8:12], AF.Exp)
                e.activation(t["GCR"][0:1, 0:4], t["LGR"][0:1, 0:4], AF.Exp,
                             scale=64.0)
                for h in range(H):
                    e.activation(t["GPAR"][0:1, h * 64:(h + 1) * 64],
                                 t["IOTAS"][0:1, :], AF.Exp,
                                 scale=t["LGR"][0:1, h:h + 1],
                                 bias=t["DRO"][0:1, 8 + h:9 + h])
                    last = e.activation(t["GPTR"][0:1, h * 64:(h + 1) * 64],
                                        t["RIOTAS"][0:1, :], AF.Exp,
                                        scale=t["LGR"][0:1, h:h + 1])
                last.then_inc(scA.h, 1)
                scA.n = b["scA"] + 1

            @blk.gpsimd
            def _(e):
                e.wait_ge(scA.h, b["scA"] + 1)
                e.dma_start(out=bass.AP(t["misc"], 0, [[64, 1], [1, 4]]),
                            in_=t["LGR"][0:1, 0:4]).then_inc(gqS.h, 16)
                e.dma_start(out=bass.AP(t["misc"], 4, [[64, 1], [1, 4]]),
                            in_=t["ALR"][0:1, 0:4]).then_inc(gqS.h, 16)
                e.dma_start(out=bass.AP(t["misc"], 8, [[64, 1], [1, 4]]),
                            in_=t["GCR"][0:1, 0:4]).then_inc(gqS.h, 16)
                for h in range(H):
                    for j in range(8):
                        e.dma_start(out=bass.AP(t["gparow"], h * 512 + j * 64,
                                                [[2048, 1], [1, 64]]),
                                    in_=t["GPAR"][0:1, h * 64:(h + 1) * 64]
                                    ).then_inc(gqS.h, 16)
                e.dma_start(out=t["gptrow"].ap(), in_=t["GPTR"][0:1, :]
                            ).then_inc(gqS.h, 16)
                gqS.n = b["gqS"] + 16 * 36
                e.wait_ge(gqS.h, gqS.n)
                e.dma_start(out=t["LGB"][:, :],
                            in_=bass.AP(t["misc"], 0, [[0, 128], [1, 4]])
                            ).then_inc(gqB.h, 16)
                e.dma_start(out=t["ALB"][:, :],
                            in_=bass.AP(t["misc"], 4, [[0, 128], [1, 4]])
                            ).then_inc(gqB.h, 16)
                e.dma_start(out=t["GC128"][:, :],
                            in_=bass.AP(t["misc"], 8, [[0, 128], [1, 4]])
                            ).then_inc(gqB.h, 16)
                for h in range(H):
                    e.dma_start(out=self.GPAT[h][:, :],
                                in_=bass.AP(t["gparow"], h * 512,
                                            [[0, 128], [1, 512]])
                                ).then_inc(gqB.h, 16)
                for h in range(H):
                    e.dma_start(out=t["GT128"][0:64, h:h + 1],
                                in_=bass.AP(t["gptrow"], h * 64, [[1, 64], [1, 1]])
                                ).then_inc(gqB.h, 16)
                    e.dma_start(out=t["GT128"][64:128, h:h + 1],
                                in_=bass.AP(t["gptrow"], h * 64, [[1, 64], [1, 1]])
                                ).then_inc(gqB.h, 16)
                gqB.n = b["gqB"] + 16 * 15

    def p0_masks(self):
        nc, t = self.nc, self.t
        sqL, scA, veV = self.sqL, self.scA, self.veV
        b = self.bases()
        with nc.Block() as blk:
            @blk.sync
            def _(e):
                e.dma_start(out=t["CG0"][:, :], in_=t["cdiffsg"].ap()
                            ).then_inc(sqL.h, 16)
                e.wait_ge(scA.h, b["scA"] + 4)
                e.dma_start(out=t["CG0"][:, :], in_=t["cnstrictg"].ap()
                            ).then_inc(sqL.h, 16)
                e.wait_ge(veV.h, b["veV"] + 1)
                e.dma_start(out=t["CG0"][:, :], in_=t["cdiffstg"].ap()
                            ).then_inc(sqL.h, 16)
                e.wait_ge(scA.h, b["scA"] + 8)
                e.dma_start(out=t["CG0"][:, :], in_=t["cnstricttg"].ap()
                            ).then_inc(sqL.h, 16)
                e.wait_ge(veV.h, b["veV"] + 2)
                e.dma_start(out=t["CG0"][:, :], in_=t["cstricttg"].ap()
                            ).then_inc(sqL.h, 16)
                e.wait_ge(veV.h, b["veV"] + 3)
                e.dma_start(out=t["TQAp"][:, 0:64], in_=t["ceye128"].ap()
                            ).then_inc(sqL.h, 16)
                e.wait_ge(veV.h, b["veV"] + 4)
                e.dma_start(out=t["TQAp"][:, 128:256], in_=t["ceyef"].ap()
                            ).then_inc(sqL.h, 16)
                sqL.n = b["sqL"] + 16 * 7

            @blk.scalar
            def _(e):
                e.wait_ge(sqL.h, b["sqL"] + 16 * 1)
                e.wait_ge(self.gqB.h, b["gqB"])
                for h in range(H):
                    e.activation(t["CG1"][:, h * 256:(h + 1) * 256],
                                 t["CG0"][:, h * 256:(h + 1) * 256], AF.Exp,
                                 scale=t["LGB"][:, h:h + 1]).then_inc(scA.h, 1)
                e.wait_ge(sqL.h, b["sqL"] + 16 * 3)
                e.wait_ge(veV.h, b["veV"] + 1)
                for h in range(H):
                    e.activation(t["CG1"][:, h * 256:(h + 1) * 256],
                                 t["CG0"][:, h * 256:(h + 1) * 256], AF.Exp,
                                 scale=t["LGB"][:, h:h + 1]).then_inc(scA.h, 1)
                scA.n = b["scA"] + 8

            @blk.vector
            def _(e):
                e.wait_ge(sqL.h, b["sqL"] + 16 * 2)
                e.wait_ge(scA.h, b["scA"] + 4)
                e.tensor_tensor(t["NEGLS"][:, :], t["CG1"][:, :], t["CG0"][:, :],
                                OP.mult).then_inc(veV.h, 1)
                e.wait_ge(sqL.h, b["sqL"] + 16 * 4)
                e.wait_ge(scA.h, b["scA"] + 8)
                e.tensor_tensor(t["NEGLST"][:, :], t["CG1"][:, :], t["CG0"][:, :],
                                OP.mult).then_inc(veV.h, 1)
                e.wait_ge(sqL.h, b["sqL"] + 16 * 5)
                e.tensor_tensor(t["ALPHALMT"][:, :], t["CG1"][:, :],
                                t["CG0"][:, :], OP.mult)
                for h in range(H):
                    li = e.tensor_scalar(t["ALPHALMT"][:, h * 256:(h + 1) * 256],
                                         t["ALPHALMT"][:, h * 256:(h + 1) * 256],
                                         t["ALB"][:, h:h + 1], -1.0,
                                         OP.mult, OP.mult)
                li.then_inc(veV.h, 1)
                e.wait_ge(sqL.h, b["sqL"] + 16 * 6)
                e.tensor_copy(t["EYE128"][:, :], t["TQAp"][:, 0:64]
                              ).then_inc(veV.h, 1)
                e.wait_ge(sqL.h, b["sqL"] + 16 * 7)
                e.tensor_copy(t["EYEF"][:, :], t["TQAp"][:, 128:256])
                e.tensor_scalar_mul(t["GT128"][:, :], t["GT128"][:, :], -1.0)
                veV.n = b["veV"] + 4

    def p0_weights(self):
        self._quant_w2()
        self._quant_w1()
        self._weight_cast()

    def _quant_w2(self):
        nc, t = self.nc, self.t
        sqL, sqT, gqS, veV, scA = self.sqL, self.sqT, self.gqS, self.veV, self.scA
        mmG = self.mmG
        PB = self.PB
        b = self.bases()
        with nc.Block() as blk:
            @blk.sync
            def _(e):
                for j in range(ND):
                    buf = t["WT0"] if j % 2 == 0 else t["WT1"]
                    if j >= 2:
                        e.wait_ge(scA.h, b["scA"] + (j - 1))
                    e.dma_start(out=buf[:, :], in_=t["w2"][j * 128:(j + 1) * 128, :]
                                ).then_inc(self.dsl[j % 2].h, 16)
                    self.dsl[j % 2].n += 16

            @blk.vector
            def _(e):
                def recip(j):
                    buf = t["WT0"] if j % 2 == 0 else t["WT1"]
                    e.wait_ge(self.dsl[j % 2].h,
                              b["dsl%d" % (j % 2)] + 16 * (j // 2 + 1))
                    self.fv(e, e.reduce_max(t["SWV2"][:, j:j + 1], buf[:, :],
                                            axis=AX, apply_absolute_value=True))
                    self.fv(e, e.tensor_scalar_mul(t["SWV2"][:, j:j + 1],
                                                   t["SWV2"][:, j:j + 1],
                                                   1.0 / 127.0))
                    e.reciprocal(t["QT5"][:, j:j + 1], t["SWV2"][:, j:j + 1]
                                 ).then_inc(veV.h, 1)

                recip(0)
                recip(1)
                for j in range(ND):
                    tq = t["TQAp"] if j % 2 == 0 else t["TQBp"]
                    e.wait_ge(scA.h, b["scA"] + (j + 1))
                    e.tensor_scalar_add(self.wbuf[j][:, :], tq[:, :],
                                        -MAGIC).then_inc(veV.h, 1)
                    if j + 2 < ND:
                        recip(j + 2)
                veV.n = b["veV"] + 2 * ND

            @blk.scalar
            def _(e):
                for j in range(ND):
                    buf = t["WT0"] if j % 2 == 0 else t["WT1"]
                    tq = t["TQAp"] if j % 2 == 0 else t["TQBp"]
                    e.wait_ge(veV.h, b["veV"] + (j + 1 if j < 2 else 2 * j))
                    e.activation(tq[:, :], buf[:, :], AF.Identity,
                                 scale=t["QT5"][:, j:j + 1], bias=MAGIC
                                 ).then_inc(scA.h, 1)
                scA.n = b["scA"] + ND

        b = self.bases()
        with nc.Block() as blk:
            @blk.tensor
            def _(e):
                for j2 in range(NF):
                    for w in range(2):
                        wid = j2 * 2 + w
                        if j2 >= 2:
                            e.wait_ge(scA.h, b["scA"] + 2 * (j2 - 2) + w + 1)
                        bk = (j2 % 2) * 2 + w
                        for cq in range(4):
                            c = w * 4 + cq
                            mi = e.matmul(
                                PB[bk][:, cq * 128:(cq + 1) * 128],
                                self.wbuf[c][:, j2 * 128:(j2 + 1) * 128],
                                t["EYEF"][:, :], start=True, stop=True)
                        mi.then_inc(mmG.h, 1)
                mmG.n = b["mmG"] + 2 * NF

            @blk.scalar
            def _(e):
                for j2 in range(NF):
                    stg = t["W2T1"] if j2 % 2 == 0 else t["W2T2"]
                    for w in range(2):
                        wid = j2 * 2 + w
                        e.wait_ge(mmG.h, b["mmG"] + wid + 1)
                        if w == 0 and j2 >= 2:
                            e.wait_ge(self.dstp[j2 % 2].h,
                                      b["dstp%d" % (j2 % 2)]
                                      + 16 * ((j2 - 2) // 2 + 1))
                        bk = (j2 % 2) * 2 + w
                        e.activation(stg[:, w * 512:(w + 1) * 512],
                                     PB[bk][:, 0:512], AF.Identity
                                     ).then_inc(scA.h, 1)
                scA.n = b["scA"] + 2 * NF

            @blk.gpsimd
            def _(e):
                for j2 in range(NF):
                    stg = t["W2T1"] if j2 % 2 == 0 else t["W2T2"]
                    e.wait_ge(scA.h, b["scA"] + 2 * (j2 + 1))
                    e.dma_start(out=t["w2q"][j2 * 128:(j2 + 1) * 128, :],
                                in_=stg[:, :]).then_inc(self.dstp[j2 % 2].h, 16)
                    self.dstp[j2 % 2].n += 16

    def _quant_w1(self):
        nc, t = self.nc, self.t
        gqS, veV, scA, mmG = self.gqS, self.veV, self.scA, self.mmG
        PB = self.PB
        b = self.bases()
        with nc.Block() as blk:
            @blk.sync
            def _(e):
                for j in range(NF):
                    dst = t["WT0"][:, (j % 4) * 1024:(j % 4 + 1) * 1024]
                    if j >= 4:
                        e.wait_ge(scA.h, b["scA"] + 3 * j - 11)
                    e.dma_start(out=dst, in_=t["w1"][j * 128:(j + 1) * 128, :]
                                ).then_inc(self.dsl[j % 4].h, 16)
                    self.dsl[j % 4].n += 16

            @blk.vector
            def _(e):
                def recip(j):
                    src_ = t["WT0"][:, (j % 4) * 1024:(j % 4 + 1) * 1024]
                    e.wait_ge(self.dsl[j % 4].h,
                              b["dsl%d" % (j % 4)] + 16 * (j // 4 + 1))
                    self.fv(e, e.reduce_max(t["SWV1"][:, j:j + 1], src_,
                                            axis=AX, apply_absolute_value=True))
                    self.fv(e, e.tensor_scalar_mul(t["SWV1"][:, j:j + 1],
                                                   t["SWV1"][:, j:j + 1],
                                                   1.0 / 127.0))
                    e.reciprocal(t["QT5"][:, j:j + 1], t["SWV1"][:, j:j + 1]
                                 ).then_inc(veV.h, 1)

                recip(0)
                recip(1)
                for j in range(NF):
                    e.wait_ge(scA.h, b["scA"] + 3 * j + 1)
                    stg = t["W2T1"] if j % 2 == 0 else t["W2T2"]
                    tq = t["TQAp"][:, (j % 2) * 1024:(j % 2 + 1) * 1024]
                    if j >= 2:
                        e.wait_ge(mmG.h, b["mmG"] + 2 * j - 2)
                    e.tensor_scalar_add(stg[:, :], tq,
                                        -MAGIC).then_inc(veV.h, 1)
                    if j + 2 < NF:
                        recip(j + 2)
                veV.n = b["veV"] + 2 * NF

            @blk.scalar
            def _(e):
                for j in range(NF):
                    src_ = t["WT0"][:, (j % 4) * 1024:(j % 4 + 1) * 1024]
                    tq = t["TQAp"][:, (j % 2) * 1024:(j % 2 + 1) * 1024]
                    e.wait_ge(veV.h, b["veV"] + (j + 1 if j < 2 else 2 * j))
                    e.activation(tq, src_, AF.Identity,
                                 scale=t["QT5"][:, j:j + 1], bias=MAGIC
                                 ).then_inc(scA.h, 1)
                    for w in range(2):
                        e.wait_ge(mmG.h, b["mmG"] + 2 * j + w + 1)
                        for cq in range(4):
                            c = w * 4 + cq
                            li = e.activation(
                                self.wbuf[c][:, j * 128:(j + 1) * 128],
                                PB[(j % 2) * 4 + cq][:, 0:128], AF.Identity)
                        li.then_inc(scA.h, 1)
                scA.n = b["scA"] + 3 * NF

            @blk.tensor
            def _(e):
                def tt(j):
                    return 2 * j + 3 if j <= NF - 3 else 2 * NF - (NF - 1 - j)

                for j in range(NF):
                    stg = t["W2T1"] if j % 2 == 0 else t["W2T2"]
                    e.wait_ge(veV.h, b["veV"] + tt(j))
                    for w in range(2):
                        if w == 1:
                            e.wait_ge(scA.h, b["scA"] + 3 * j + 2)
                        elif j >= 2:
                            e.wait_ge(scA.h, b["scA"] + 3 * j - 3)
                        for cq in range(4):
                            c = w * 4 + cq
                            mi = e.matmul(PB[(j % 2) * 4 + cq][:, 0:128],
                                          stg[:, c * 128:(c + 1) * 128],
                                          t["EYEF"][:, :], start=True, stop=True)
                        mi.then_inc(mmG.h, 1)
                mmG.n = b["mmG"] + 2 * NF

            @blk.gpsimd
            def _(e):
                e.wait_ge(scA.h, b["scA"] + 3 * NF)
                for c in range(ND):
                    e.dma_start(out=t["w1q"][c * 128:(c + 1) * 128, :],
                                in_=self.wbuf[c][:, :]).then_inc(gqS.h, 16)
                gqS.n = b["gqS"] + 16 * ND

    def _weight_cast(self):
        nc, t = self.nc, self.t
        veV = self.veV
        b = self.bases()
        items = [("wwt", "wwtb", c) for c in range(ND)] + \
                [("wrt", "wrtb", c) for c in range(ND)]
        with nc.Block() as blk:
            @blk.sync
            def _(e):
                for i, (src, dst, c) in enumerate(items):
                    if i >= 4:
                        e.wait_ge(veV.h, b["veV"] + (i - 3))
                    e.dma_start(out=t["WT0"][:, (i % 4) * 1024:
                                             (i % 4 + 1) * 1024],
                                in_=t[src][c * 128:(c + 1) * 128, :]
                                ).then_inc(self.dsl[i % 4].h, 16)
                    self.dsl[i % 4].n += 16

            @blk.vector
            def _(e):
                for i, (src, dst, c) in enumerate(items):
                    e.wait_ge(self.dsl[i % 4].h,
                              b["dsl%d" % (i % 4)] + 16 * (i // 4 + 1))
                    if i >= 4:
                        e.wait_ge(self.hts[i % 4].h,
                                  b["hts%d" % (i % 4)]
                                  + 16 * ((i - 4) // 4 + 1))
                    e.tensor_copy(t["WCST"][:, (i % 4) * 1024:
                                            (i % 4 + 1) * 1024],
                                  t["WT0"][:, (i % 4) * 1024:
                                           (i % 4 + 1) * 1024]
                                  ).then_inc(veV.h, 1)
                veV.n = b["veV"] + len(items)

            @blk.gpsimd
            def _(e):
                for i, (src, dst, c) in enumerate(items):
                    e.wait_ge(veV.h, b["veV"] + (i + 1))
                    e.dma_start(out=t[dst][c * 128:(c + 1) * 128, :],
                                in_=t["WCST"][:, (i % 4) * 1024:
                                              (i % 4 + 1) * 1024]
                                ).then_inc(self.hts[i % 4].h, 16)
                    self.hts[i % 4].n += 16

    def stats_ln(self, src, with_max, per_head, ccidx=None):
        nc, t = self.nc, self.t
        gqS, gqB, ccS, mmG, veV, scA = (self.gqS, self.gqB, self.ccS, self.mmG,
                                        self.veV, self.scA)
        snG = self.snG
        PB = self.PB
        b = self.bases()
        SRC = t[src]
        dln, drn = self.dln, self.drn
        nA = 3 + (3 if per_head else 0)
        nV = 3 + (1 if with_max else 0) + (2 if per_head else 0)
        nM = 1 + (1 if per_head else 0)
        with nc.Block() as blk:
            @blk.sync
            def _(e):
                self.entry_waits(e, b)
                for g in range(NTG):
                    if g >= 2:
                        e.wait_ge(veV.h, b["veV"] + nV * (g - 1))
                        e.wait_ge(scA.h, b["scA"] + nA * (g - 2) + 2)
                    xa = t["XA0"] if g % 2 == 0 else t["XA1"]
                    e.dma_start(out=xa[:, :],
                                in_=bass.AP(SRC, g * TG,
                                            [[R, 128], [128 * R, ND], [1, TG]])
                                ).then_inc(self.dsl[g % 2].h, 16)
                    self.dsl[g % 2].n += 16

            @blk.scalar
            def _(e):
                for g in range(NTG):
                    xa = t["XA0"] if g % 2 == 0 else t["XA1"]
                    tb = t["TB0"] if g % 2 == 0 else t["TB1"]
                    vb = t["VB0"] if g % 2 == 0 else t["VB1"]
                    e.wait_ge(self.dsl[g % 2].h,
                              b["dsl%d" % (g % 2)] + 16 * (g // 2 + 1))
                    if g >= 1:
                        if per_head:
                            e.wait_ge(drn.h, b["drn"] + 32 * g)
                        else:
                            e.wait_ge(mmG.h, b["mmG"] + nM * g)
                    e.activation(t["XBF"][:, :], xa[:, :], AF.Identity
                                 ).then_inc(scA.h, 1)
                    e.activation(t["XSQ"][:, :], xa[:, :], AF.Square
                                 ).then_inc(scA.h, 1)
                    e.wait_ge(veV.h, b["veV"] + nV * g + 1)
                    e.activation(tb[:, :], vb[:, :], AF.Sqrt).then_inc(scA.h, 1)
                    if per_head:
                        e.wait_ge(veV.h, b["veV"] + nV * g + 3)
                        e.wait_ge(mmG.h, b["mmG"] + nM * g + 1)
                        e.activation(t["XSQ"][:, :], t["LNT0"][:, :], AF.Square
                                     ).then_inc(scA.h, 1)
                        e.wait_ge(mmG.h, b["mmG"] + nM * g + 2)
                        if g >= 1:
                            e.wait_ge(veV.h, b["veV"] + nV * (g - 1) + 5)
                        for hh in range(H):
                            li = e.activation(t["RN%d" % hh][:, :],
                                              PB[2 + hh][:, :], AF.Sqrt)
                        li.then_inc(scA.h, 1)
                        e.activation(t["UBT"][:, :], t["LNT0"][:, :],
                                     AF.Identity).then_inc(scA.h, 1)
                scA.n = b["scA"] + nA * NTG

            @blk.vector
            def _(e):
                for g in range(NTG):
                    xa = t["XA0"] if g % 2 == 0 else t["XA1"]
                    mb = t["MB0"] if g % 2 == 0 else t["MB1"]
                    vb = t["VB0"] if g % 2 == 0 else t["VB1"]
                    tb = t["TB0"] if g % 2 == 0 else t["TB1"]
                    rb = t["RB0"] if g % 2 == 0 else t["RB1"]
                    mrb = t["MRB0"] if g % 2 == 0 else t["MRB1"]
                    e.wait_ge(mmG.h, b["mmG"] + nM * g + 1)
                    e.tensor_scalar_mul(mb[:, :], PB[0][:, :], 1.0 / D)
                    e.tensor_tensor(tb[:, :], mb[:, :], mb[:, :], OP.mult)
                    e.tensor_scalar_mul(vb[:, :], PB[1][:, :], 1.0 / D)
                    e.tensor_sub(vb[:, :], vb[:, :], tb[:, :])
                    e.tensor_scalar_add(vb[:, :], vb[:, :], EPSLN
                                        ).then_inc(veV.h, 1)
                    e.wait_ge(scA.h, b["scA"] + nA * g + 3)
                    self.fv(e, e.reciprocal(rb[:, :], tb[:, :]))
                    e.tensor_tensor(mrb[:, :], mb[:, :], rb[:, :], OP.mult
                                    ).then_inc(veV.h, 1)
                    if g >= 1:
                        e.wait_ge(dln.h, b["dln"] + 16 * g)
                        if per_head:
                            e.wait_ge(scA.h, b["scA"] + nA * (g - 1) + 6)
                    for c in range(ND):
                        e.tensor_tensor(t["LNT0"][:, c * TG:(c + 1) * TG],
                                        xa[:, c * TG:(c + 1) * TG], rb[:, :],
                                        OP.mult)
                    for c in range(ND):
                        ln_i = e.tensor_tensor(t["LNT0"][:, c * TG:(c + 1) * TG],
                                               t["LNT0"][:, c * TG:(c + 1) * TG],
                                               mrb[:, :], OP.subtract)
                    ln_i.then_inc(veV.h, 1)
                    if with_max:
                        self.fv(e, e.reduce_max(t["TMPC"][:, 0:1],
                                                t["LNT0"][:, :], axis=AX))
                        e.tensor_max(t["RMAX"][:, 0:1], t["RMAX"][:, 0:1],
                                     t["TMPC"][:, 0:1]).then_inc(veV.h, 1)
                    if per_head:
                        e.wait_ge(scA.h, b["scA"] + nA * g + 5)
                        self.fv(e, e.tensor_scalar_max(t["RNALL"][:, :],
                                                       t["RNALL"][:, :],
                                                       1e-12))
                        e.reciprocal(t["RNALL"][:, :], t["RNALL"][:, :]
                                     ).then_inc(veV.h, 1)
                        if g >= 1:
                            e.wait_ge(drn.h, b["drn"] + 32 * g - 16)
                        for c in range(ND):
                            li = e.tensor_tensor(
                                t["XSQ"][:, c * TG:(c + 1) * TG],
                                t["LNT0"][:, c * TG:(c + 1) * TG],
                                t["RN%d" % (c // 2)][:, :], OP.mult)
                        li.then_inc(veV.h, 1)
                if ccidx is not None:
                    e.wait_ge(gqB.h, b["gqB"] + 16)
                    self.fv(e, e.memset(t["CCS"][0:1, :], NEGBIG))
                    e.reduce_max(t["CCS"][0:1, 0:1], t["MROWS"][0:1, :],
                                 axis=AX).then_inc(veV.h, 1)
                    e.wait_ge(gqB.h, b["gqB"] + 32)
                    self.fv(e, e.tensor_scalar_max(t["S1C"][:, 0:1],
                                                   t["SCC128"][:, 0:1], EPSQ))
                    self.fv(e, e.tensor_scalar_mul(t["S1C"][:, 0:1],
                                                   t["S1C"][:, 0:1],
                                                   1.0 / 255.0))
                    e.reciprocal(t["IS1C"][:, 0:1], t["S1C"][:, 0:1]
                                 ).then_inc(veV.h, 1)
                    e.memset(t["RMAX"][:, :], NEGBIG)
                    veV.n = b["veV"] + nV * NTG + 2
                else:
                    veV.n = b["veV"] + nV * NTG

            @blk.tensor
            def _(e):
                for g in range(NTG):
                    e.wait_ge(scA.h, b["scA"] + nA * g + 2)
                    if g >= 1:
                        e.wait_ge(veV.h, b["veV"] + nV * (g - 1) + 1)
                    for c in range(ND):
                        e.matmul(PB[0][:, :], t["ONES128"][:, :],
                                 t["XBF"][:, c * TG:(c + 1) * TG],
                                 start=(c == 0), stop=(c == ND - 1))
                    for c in range(ND):
                        mi = e.matmul(PB[1][:, :], t["ONES128"][:, :],
                                      t["XSQ"][:, c * TG:(c + 1) * TG],
                                      start=(c == 0), stop=(c == ND - 1))
                    mi.then_inc(mmG.h, 1)
                    if per_head:
                        e.wait_ge(scA.h, b["scA"] + nA * g + 4)
                        if g >= 1:
                            e.wait_ge(veV.h, b["veV"] + nV * (g - 1) + 5)
                        for hh in range(H):
                            for j in range(2):
                                c = 2 * hh + j
                                mi = e.matmul(PB[2 + hh][:, :],
                                              t["ONES128"][:, :],
                                              t["XSQ"][:, c * TG:(c + 1) * TG],
                                              start=(j == 0), stop=(j == 1))
                        mi.then_inc(mmG.h, 1)
                mmG.n = b["mmG"] + nM * NTG

            @blk.gpsimd
            def _(e):
                for g in range(NTG):
                    e.wait_ge(veV.h, b["veV"] + nV * g + 3)
                    e.dma_start(out=bass.AP(t["lnbuf"], g * TG,
                                            [[R, 128], [128 * R, ND], [1, TG]]),
                                in_=t["LNT0"][:, :]).then_inc(dln.h, 16)
                    dln.n += 16
                    if ccidx is not None and g == NTG - 1:
                        cci = t["cci%d" % ccidx]
                        cco = t["cco%d" % ccidx]
                        e.wait_ge(veV.h, b["veV"] + nV * NTG)
                        e.dma_start(out=t["MROWS"][0:1, :],
                                    in_=t["RMAX"][:, 0:1]).then_inc(gqB.h, 16)
                        gqB.n += 16
                        e.wait_ge(veV.h, b["veV"] + nV * NTG + 1)
                        e.dma_start(out=cci.ap(), in_=t["CCS"][0:1, :]
                                    ).then_inc(gqS.h, 16)
                        gqS.n += 16
                        e.wait_ge(gqS.h, gqS.n)
                        e.collective_compute(
                            "AllReduce", OP.max,
                            replica_groups=[list(range(8))],
                            ins=[cci.ap().opt()],
                            outs=[cco.ap().opt()],
                        ).then_inc(ccS.h, 1)
                        ccS.n += 1
                        e.wait_ge(ccS.h, ccS.n)
                        e.dma_start(out=t["SCC128"][:, 0:1],
                                    in_=bass.AP(cco, 0, [[0, 128], [1, 1]])
                                    ).then_inc(gqB.h, 16)
                        gqB.n += 16
                    if per_head:
                        e.wait_ge(veV.h, b["veV"] + nV * g + 5)
                        e.dma_start(out=bass.AP(t["rkbuf"], g * TG,
                                                [[R, 128], [128 * R, ND],
                                                 [1, TG]]),
                                    in_=t["XSQ"][:, :]).then_inc(drn.h, 16)
                        e.wait_ge(scA.h, b["scA"] + nA * g + 6)
                        e.dma_start(out=bass.AP(t["ubbuf"], g * TG,
                                                [[R, 128], [128 * R, ND],
                                                 [1, TG]]),
                                    in_=t["UBT"][:, :]).then_inc(drn.h, 16)
                        drn.n += 32

    def qscale(self, ccidx, maxsrc, swname, ntile, ssw, bsrc, bq, with_half):
        nc, t = self.nc, self.t
        gqS, gqB, ccS, veV, scA = self.gqS, self.gqB, self.ccS, self.veV, self.scA
        b = self.bases()
        cci, cco = t["cci%d" % ccidx], t["cco%d" % ccidx]
        with nc.Block() as blk:
            @blk.gpsimd
            def _(e):
                e.dma_start(out=bass.AP(t["mrow"], 0, [[1, 128], [1, 1]]),
                            in_=t[maxsrc][:, 0:1]).then_inc(gqS.h, 16)
                e.wait_ge(gqS.h, b["gqS"] + 16)
                e.dma_start(out=t["MROWS"][0:1, :], in_=t["mrow"].ap()
                            ).then_inc(gqB.h, 16)
                e.wait_ge(veV.h, b["veV"] + 1)
                e.dma_start(out=cci.ap(), in_=t["CCS"][0:1, :]).then_inc(gqS.h, 16)
                e.wait_ge(gqS.h, b["gqS"] + 32)
                e.collective_compute(
                    "AllReduce", OP.max,
                    replica_groups=[list(range(8))],
                    ins=[cci.ap().opt()],
                    outs=[cco.ap().opt()],
                ).then_inc(ccS.h, 1)
                e.wait_ge(ccS.h, b["ccS"] + 1)
                e.dma_start(out=t["SCC"][0:1, :], in_=cco.ap()).then_inc(gqB.h, 16)
                gqS.n = b["gqS"] + 32
                gqB.n = b["gqB"] + 32
                ccS.n = b["ccS"] + 1
                e.wait_ge(veV.h, b["veV"] + 2)
                e.dma_start(out=bass.AP(t["misc2"], 0, [[2, 1], [1, 2]]),
                            in_=t["TB0"][0:1, 0:2]).then_inc(gqS.h, 16)
                gqS.n += 16
                e.wait_ge(gqS.h, gqS.n)
                e.dma_start(out=t["S1C"][:, 0:1],
                            in_=bass.AP(t["misc2"], 0, [[0, 128], [1, 1]])
                            ).then_inc(gqB.h, 16)
                e.dma_start(out=t["IS1C"][:, 0:1],
                            in_=bass.AP(t["misc2"], 1, [[0, 128], [1, 1]])
                            ).then_inc(gqB.h, 16)
                gqB.n += 32

            @blk.vector
            def _(e):
                e.wait_ge(gqB.h, b["gqB"] + 16)
                self.fv(e, e.memset(t["CCS"][0:1, :], NEGBIG))
                e.reduce_max(t["CCS"][0:1, 0:1], t["MROWS"][0:1, :], axis=AX
                             ).then_inc(veV.h, 1)
                e.wait_ge(gqB.h, b["gqB"] + 32)
                self.fv(e, e.tensor_scalar_max(t["TB0"][0:1, 0:1],
                                               t["SCC"][0:1, 0:1], EPSQ))
                self.fv(e, e.tensor_scalar_mul(t["TB0"][0:1, 0:1],
                                               t["TB0"][0:1, 0:1], 1.0 / 255.0))
                e.reciprocal(t["TB0"][0:1, 1:2], t["TB0"][0:1, 0:1]
                             ).then_inc(veV.h, 1)
                e.wait_ge(gqB.h, b["gqB"] + 64)
                self.fv(e, e.tensor_scalar_mul(t[ssw][:, 0:ntile],
                                               t[swname][:, 0:ntile],
                                               t["S1C"][:, 0:1]))
                self.fv(e, e.reciprocal(t["QT5"][:, 0:ntile], t[ssw][:, 0:ntile]))
                self.fv(e, e.tensor_tensor(t["QT6"][:, 0:ntile],
                                           t[bsrc][:, 0:ntile],
                                           t["QT5"][:, 0:ntile], OP.mult))
                self.fv(e, e.tensor_scalar_add(t["QT7"][:, 0:ntile],
                                               t["QT6"][:, 0:ntile], MAGIC))
                e.tensor_scalar_add(t["QT7"][:, 0:ntile], t["QT7"][:, 0:ntile],
                                    -MAGIC).then_inc(veV.h, 1)
                e.wait_ge(scA.h, b["scA"] + 1)
                self.fv(e, e.tensor_single_scalar(t["QTM"][:, 0:ntile],
                                                  t["QT8"][:, 0:ntile],
                                                  RRLIM, OP.is_lt))
                self.fv(e, e.select(t["QT6"][:, 0:ntile], t["QTM"][:, 0:ntile],
                                    t["QT7"][:, 0:ntile], t["QT6"][:, 0:ntile]))
                self.fv(e, e.tensor_tensor(t[bq][:, 0:ntile], t["QT6"][:, 0:ntile],
                                           t[ssw][:, 0:ntile], OP.mult))
                if with_half:
                    self.fv(e, e.tensor_scalar_mul(t[ssw][:, 0:ntile],
                                                   t[ssw][:, 0:ntile], 0.5))
                    li = e.tensor_scalar_mul(t[bq][:, 0:ntile], t[bq][:, 0:ntile],
                                             0.5)
                else:
                    li = e.tensor_scalar_mul(t[bq][:, 0:ntile], t[bq][:, 0:ntile],
                                             1.0)
                li.then_inc(veV.h, 1)
                veV.n = b["veV"] + 4

            @blk.scalar
            def _(e):
                e.wait_ge(veV.h, b["veV"] + 3)
                e.activation(t["QT8"][:, 0:ntile], t["QT6"][:, 0:ntile], AF.Abs
                             ).then_inc(scA.h, 1)
                scA.n = b["scA"] + 1

    def mat1(self, ccidx=None):
        nc, t = self.nc, self.t
        sqL, gqS, gqB, ccS, mmG, veV, scA = (self.sqL, self.gqS, self.gqB,
                                             self.ccS, self.mmG, self.veV,
                                             self.scA)
        PB = self.PB
        b = self.bases()
        vO = 1

        def act2_t(g):
            return 2 * (g + 1) + 32 * max(g - 1, 0)

        def evac_t(gi):
            g, ft = divmod(gi, NF)
            if g < NTG - 1:
                return 2 * (g + 2) + 32 * g + ft + 1
            return 2 * NTG + 32 * (NTG - 1) + ft + 1

        def emit_evacs(e, g):
            for ft in range(NF):
                gi = g * NF + ft
                e.wait_ge(mmG.h, b["mmG"] + gi + 1)
                if gi == 0:
                    e.wait_ge(veV.h, b["veV"] + vO)
                if gi >= 4:
                    e.wait_ge(self.hts[ft % 4].h,
                              b["hts%d" % (ft % 4)] + 16 * (gi // 4))
                    e.wait_ge(veV.h, b["veV"] + vO + (gi - 3))
                e.activation(t["HT%d" % (ft % 4)][:, :], PB[ft % 4][:, :],
                             AF.Relu, scale=t["S1SW"][:, ft:ft + 1],
                             bias=t["B1Q"][:, ft:ft + 1]).then_inc(scA.h, 1)

        with nc.Block() as blk:
            @blk.sync
            def _(e):
                self.entry_waits(e, b)
                for g in range(NTG):
                    if g >= 2:
                        e.wait_ge(scA.h, b["scA"] + act2_t(g - 2))
                    xa = t["MA0"] if g % 2 == 0 else t["MA1"]
                    e.dma_start(out=xa[:, :],
                                in_=bass.AP(t["lnbuf"], g * TG,
                                            [[R, 128], [128 * R, ND], [1, TG]])
                                ).then_inc(self.dsl[g % 2].h, 16)
                    self.dsl[g % 2].n += 16

            @blk.scalar
            def _(e):
                for g in range(NTG):
                    xa = t["MA0"] if g % 2 == 0 else t["MA1"]
                    qx = t["QX0"] if g % 2 == 0 else t["QX1"]
                    e.wait_ge(self.dsl[g % 2].h,
                              b["dsl%d" % (g % 2)] + 16 * (g // 2 + 1))
                    if g >= 2:
                        e.wait_ge(mmG.h, b["mmG"] + 32 * (g - 1))
                    e.activation(t["MTQ"][:, :], xa[:, :], AF.Identity,
                                 scale=t["IS1C"][:, 0:1], bias=MAGIC
                                 ).then_inc(scA.h, 1)
                    e.activation(qx[:, :], t["MTQ"][:, :], AF.Relu, bias=-MAGIC
                                 ).then_inc(scA.h, 1)
                    if g >= 1:
                        emit_evacs(e, g - 1)
                emit_evacs(e, NTG - 1)
                scA.n = b["scA"] + 34 * NTG

            @blk.tensor
            def _(e):
                e.wait_ge(self.sqL.h, b["sqL"])
                for g in range(NTG):
                    qx = t["QX0"] if g % 2 == 0 else t["QX1"]
                    e.wait_ge(scA.h, b["scA"] + act2_t(g))
                    for ft in range(NF):
                        gi = g * NF + ft
                        if gi >= 4:
                            e.wait_ge(scA.h, b["scA"] + evac_t(gi - 4))
                        for k in range(ND):
                            mi = e.matmul(PB[ft % 4][:, :],
                                          self.wbuf[k][:, ft * 128:(ft + 1) * 128],
                                          qx[:, k * TG:(k + 1) * TG],
                                          start=(k == 0), stop=(k == ND - 1))
                        mi.then_inc(mmG.h, 1)
                mmG.n = b["mmG"] + 32 * NTG

            @blk.vector
            def _(e):
                self.derive_q(e, "SWV1", 32, "S1SW", "B1S", "B1Q", False)
                e.tensor_copy(t["TMPC"][:, 0:1], t["TMPC"][:, 0:1]
                              ).then_inc(veV.h, 1)
                for g in range(NTG):
                    for ft in range(NF):
                        gi = g * NF + ft
                        e.wait_ge(scA.h, b["scA"] + evac_t(gi))
                        self.fv(e, e.reduce_max(t["TMPC"][:, 0:1],
                                                t["HT%d" % (ft % 4)][:, :],
                                                axis=AX))
                        e.tensor_max(t["HMAX"][:, 0:1], t["HMAX"][:, 0:1],
                                     t["TMPC"][:, 0:1]).then_inc(veV.h, 1)
                if ccidx is not None:
                    e.wait_ge(gqB.h, b["gqB"] + 16)
                    self.fv(e, e.memset(t["CCS"][0:1, :], NEGBIG))
                    e.reduce_max(t["CCS"][0:1, 0:1], t["MROWS"][0:1, :],
                                 axis=AX).then_inc(veV.h, 1)
                    e.wait_ge(gqB.h, b["gqB"] + 32)
                    self.fv(e, e.tensor_scalar_max(t["S1C"][:, 0:1],
                                                   t["SCC128"][:, 0:1], EPSQ))
                    self.fv(e, e.tensor_scalar_mul(t["S1C"][:, 0:1],
                                                   t["S1C"][:, 0:1],
                                                   1.0 / 255.0))
                    e.reciprocal(t["IS1C"][:, 0:1], t["S1C"][:, 0:1]
                                 ).then_inc(veV.h, 1)
                    e.memset(t["HMAX"][:, :], NEGBIG)
                    veV.n = b["veV"] + vO + 32 * NTG + 2
                else:
                    veV.n = b["veV"] + vO + 32 * NTG

            @blk.gpsimd
            def _(e):
                for g in range(NTG):
                    for ft in range(NF):
                        gi = g * NF + ft
                        e.wait_ge(scA.h, b["scA"] + evac_t(gi))
                        e.dma_start(out=t["hbuf"][ft * 128:(ft + 1) * 128,
                                                  g * TG:(g + 1) * TG],
                                    in_=t["HT%d" % (ft % 4)][:, :]
                                    ).then_inc(self.hts[ft % 4].h, 16)
                        self.hts[ft % 4].n += 16
                if ccidx is not None:
                    cci = t["cci%d" % ccidx]
                    cco = t["cco%d" % ccidx]
                    e.wait_ge(veV.h, b["veV"] + vO + 32 * NTG)
                    e.dma_start(out=t["MROWS"][0:1, :],
                                in_=t["HMAX"][:, 0:1]).then_inc(gqB.h, 16)
                    gqB.n += 16
                    e.wait_ge(veV.h, b["veV"] + vO + 32 * NTG + 1)
                    e.dma_start(out=cci.ap(), in_=t["CCS"][0:1, :]
                                ).then_inc(gqS.h, 16)
                    gqS.n += 16
                    e.wait_ge(gqS.h, gqS.n)
                    e.collective_compute(
                        "AllReduce", OP.max,
                        replica_groups=[list(range(8))],
                        ins=[cci.ap().opt()],
                        outs=[cco.ap().opt()],
                    ).then_inc(ccS.h, 1)
                    ccS.n += 1
                    e.wait_ge(ccS.h, ccS.n)
                    e.dma_start(out=t["SCC128"][:, 0:1],
                                in_=bass.AP(cco, 0, [[0, 128], [1, 1]])
                                ).then_inc(gqB.h, 16)
                    gqB.n += 16

    def mat2(self, xsrc, dst):
        nc, t = self.nc, self.t
        sqL, gqS, gqB, mmG, veV, scA = (self.sqL, self.gqS, self.gqB, self.mmG,
                                        self.veV, self.scA)
        PB = self.PB
        b = self.bases()
        NG2 = 16
        T2 = 256
        vO = 1

        def act1_t(g, q):
            return (q + 1) if g == 0 else 16 * g - 8 + q + 1

        def evac_t(g, dt):
            return (16 * g + 16 + dt + 1) if g < NG2 - 1 \
                else 16 * NG2 - 8 + dt + 1

        with nc.Block() as blk:
            @blk.sync
            def _(e):
                self.entry_waits(e, b)
                for g in range(NG2):
                    for q in range(8):
                        pg, pq = (g, q - 4) if q >= 4 else (g - 1, q + 4)
                        if pg >= 0:
                            e.wait_ge(scA.h, b["scA"] + act1_t(pg, pq))
                        e.dma_start(out=t["HTQ4"][:, (q % 4) * 1024:
                                                  (q % 4 + 1) * 1024],
                                    in_=bass.AP(t["hbuf"],
                                                q * 4 * 128 * R + g * T2,
                                                [[R, 128], [128 * R, 4], [1, T2]])
                                    ).then_inc(self.dsl[q % 4].h, 16)
                        self.dsl[q % 4].n += 16
                    xr = t["XR0"] if g % 2 == 0 else t["XR1"]
                    if g >= 2:
                        e.wait_ge(veV.h, b["veV"] + vO + 16 * (g - 1) + 8)
                    e.dma_start(out=xr[:, :],
                                in_=bass.AP(t[xsrc], g * T2,
                                            [[R, 128], [128 * R, ND], [1, T2]])
                                ).then_inc(self.dsb[g % 2].h, 16)
                    self.dsb[g % 2].n += 16

            @blk.scalar
            def _(e):
                def qh_t(g, q):
                    return (q + 1) if g == 0 else 16 * g - 8 + q + 1

                def acts(g):
                    for q in range(8):
                        e.wait_ge(self.dsl[q % 4].h,
                                  b["dsl%d" % (q % 4)]
                                  + 16 * ((g * 8 + q) // 4 + 1))
                        pg, pq = (g, q - 2) if q >= 2 else (g - 1, q + 6)
                        if pg >= 0:
                            e.wait_ge(veV.h, b["veV"] + vO + qh_t(pg, pq))
                        e.activation(t["ATQ"][:, (q % 2) * 1024:(q % 2 + 1) * 1024],
                                     t["HTQ4"][:, (q % 4) * 1024:(q % 4 + 1) * 1024],
                                     AF.Identity, scale=t["IS1C"][:, 0:1],
                                     bias=MAGIC).then_inc(scA.h, 1)

                def res_t(g, dt):
                    return (16 * (g + 1) + dt + 1) if g < NG2 - 1 \
                        else 16 * (NG2 - 1) + 8 + dt + 1

                def evacs(g):
                    for dt in range(ND):
                        gi = g * ND + dt
                        e.wait_ge(mmG.h, b["mmG"] + gi + 1)
                        if gi == 0:
                            e.wait_ge(veV.h, b["veV"] + vO)
                        if dt >= 4:
                            e.wait_ge(veV.h, b["veV"] + vO + res_t(g, dt - 4))
                        elif g >= 1:
                            e.wait_ge(veV.h, b["veV"] + vO + res_t(g - 1, dt + 4))
                        e.activation(t["YT%d" % (dt % 4)][:, :],
                                     PB[4 + dt % 4][:, 0:T2],
                                     AF.Identity, scale=t["S2SW"][:, dt:dt + 1],
                                     bias=t["B2Q"][:, dt:dt + 1]).then_inc(scA.h, 1)

                for g in range(NG2):
                    acts(g)
                    if g >= 1:
                        evacs(g - 1)
                evacs(NG2 - 1)
                scA.n = b["scA"] + 16 * NG2

            @blk.vector
            def _(e):
                self.derive_q(e, "SWV2", 8, "S2SW", "B2S", "B2Q", True)
                e.tensor_copy(t["TMPC"][:, 0:1], t["TMPC"][:, 0:1]
                              ).then_inc(veV.h, 1)

                def qhs(g):
                    qh = t["QH0"] if g % 2 == 0 else t["QH1"]
                    for q in range(8):
                        e.wait_ge(scA.h, b["scA"] + act1_t(g, q))
                        if g >= 2 and q == 0:
                            e.wait_ge(mmG.h, b["mmG"] + 8 * (g - 1))
                        e.tensor_scalar_add(qh[:, q * 1024:(q + 1) * 1024],
                                            t["ATQ"][:, (q % 2) * 1024:
                                                     (q % 2 + 1) * 1024],
                                            -MAGIC).then_inc(veV.h, 1)

                def residuals(g):
                    xr = t["XR0"] if g % 2 == 0 else t["XR1"]
                    xo = t["XO0"] if g % 2 == 0 else t["XO1"]
                    for dt in range(ND):
                        e.wait_ge(scA.h, b["scA"] + evac_t(g, dt))
                        if dt == 0:
                            e.wait_ge(self.dsb[g % 2].h,
                                      b["dsb%d" % (g % 2)] + 16 * (g // 2 + 1))
                            if g >= 2:
                                e.wait_ge(self.dstp[g % 2].h,
                                          b["dstp%d" % (g % 2)]
                                          + 16 * ((g - 2) // 2 + 1))
                        e.tensor_add(xo[:, dt * T2:(dt + 1) * T2],
                                     xr[:, dt * T2:(dt + 1) * T2],
                                     t["YT%d" % (dt % 4)][:, :]).then_inc(veV.h, 1)

                for g in range(NG2):
                    qhs(g)
                    if g >= 1:
                        residuals(g - 1)
                residuals(NG2 - 1)
                veV.n = b["veV"] + vO + 16 * NG2

            @blk.tensor
            def _(e):
                e.wait_ge(self.sqL.h, b["sqL"])
                for g in range(NG2):
                    qh = t["QH0"] if g % 2 == 0 else t["QH1"]
                    for dt in range(ND):
                        gi = g * ND + dt
                        e.wait_ge(veV.h, b["veV"] + vO
                                  + (8 if g == 0 else 16 * g))
                        if gi >= 4:
                            pg, pdt = divmod(gi - 4, ND)
                            e.wait_ge(scA.h, b["scA"] + evac_t(pg, pdt))
                        for k in range(NF):
                            mi = e.matmul(PB[4 + dt % 4][:, 0:T2],
                                          self.wbuf[k // 4][:, (k % 4) * 1024
                                                            + dt * 128:
                                                            (k % 4) * 1024
                                                            + (dt + 1) * 128],
                                          qh[:, k * T2:(k + 1) * T2],
                                          start=(k == 0), stop=(k == NF - 1))
                        mi.then_inc(mmG.h, 1)
                mmG.n = b["mmG"] + 8 * NG2

            @blk.gpsimd
            def _(e):
                for g in range(NG2):
                    xo = t["XO0"] if g % 2 == 0 else t["XO1"]
                    rt = 16 * (g + 1) + 8 if g < NG2 - 1 else 16 * NG2
                    e.wait_ge(veV.h, b["veV"] + vO + rt)
                    e.dma_start(out=bass.AP(t[dst], g * T2,
                                            [[R, 128], [128 * R, ND], [1, T2]]),
                                in_=xo[:, :]).then_inc(self.dstp[g % 2].h, 16)
                    self.dstp[g % 2].n += 16

    def load_wq(self, which):
        nc, t = self.nc, self.t
        sqL = self.sqL
        b = self.bases()
        with nc.Block() as blk:
            @blk.sync
            def _(e):
                self.entry_waits(e, b)
                n = 0
                if which == "w1q":
                    for c in range(ND):
                        e.dma_start(out=self.wbuf[c][:, :],
                                    in_=t["w1q"][c * 128:(c + 1) * 128, :]
                                    ).then_inc(sqL.h, 16)
                        n += 16
                elif which == "w2q":
                    for i in range(ND):
                        e.dma_start(out=self.wbuf[i][:, :],
                                    in_=bass.AP(t["w2q"], i * 4 * 128 * D,
                                                [[D, 128], [128 * D, 4], [1, D]])
                                    ).then_inc(sqL.h, 16)
                        n += 16
                else:
                    for c in range(ND):
                        e.dma_start(out=self.wbuf[c][:, 0:1024],
                                    in_=t["wwtb"][c * 128:(c + 1) * 128, :]
                                    ).then_inc(sqL.h, 16)
                        e.dma_start(out=self.wbuf[c][:, 1024:2048],
                                    in_=t["wrtb"][c * 128:(c + 1) * 128, :]
                                    ).then_inc(sqL.h, 16)
                        n += 32
                sqL.n = b["sqL"] + n

    def reset_max(self, name):
        nc, t = self.nc, self.t
        with nc.Block() as blk:
            @blk.vector
            def _(e):
                e.memset(t[name][:, :], NEGBIG)

    def copy_dram(self, src, dst):
        nc, t = self.nc, self.t
        sqL, gqS = self.sqL, self.gqS
        b = self.bases()
        with nc.Block() as blk:
            @blk.sync
            def _(e):
                self.entry_waits(e, b)
                for g in range(NTG):
                    if g >= 2:
                        e.wait_ge(self.dstp[g % 2].h,
                                  b["dstp%d" % (g % 2)] + 16 * ((g - 2) // 2 + 1))
                    xa = t["XA0"] if g % 2 == 0 else t["XA1"]
                    e.dma_start(out=xa[:, :],
                                in_=bass.AP(t[src], g * TG,
                                            [[R, 128], [128 * R, ND], [1, TG]])
                                ).then_inc(self.dsl[g % 2].h, 16)
                    self.dsl[g % 2].n += 16

            @blk.gpsimd
            def _(e):
                for g in range(NTG):
                    xa = t["XA0"] if g % 2 == 0 else t["XA1"]
                    e.wait_ge(self.dsl[g % 2].h,
                              b["dsl%d" % (g % 2)] + 16 * (g // 2 + 1))
                    e.dma_start(out=bass.AP(t[dst], g * TG,
                                            [[R, 128], [128 * R, ND], [1, TG]]),
                                in_=xa[:, :]).then_inc(self.dstp[g % 2].h, 16)
                    self.dstp[g % 2].n += 16

    def finish(self):
        nc = self.nc
        b = self.bases()
        sems = ([self.gqS, self.dln, self.drn] + self.dstp + self.hts
                + self.dtp + self.dsb)
        with nc.Block() as blk:
            @blk.gpsimd
            def _(e):
                for s_ in sems:
                    e.wait_ge(s_.h, s_.n)

    def ffn(self, src, dst, cc0, cc1):
        self.stats_ln(src, with_max=True, per_head=False, ccidx=cc0)
        self.mat1(ccidx=cc1)
        self.load_wq("w2q")
        self.mat2(src, dst)

    def build(self, upto=99):
        self.declare()
        ph = 0
        steps = [
            self.p0_scalars,
            self.p0_masks,
            lambda: self._quant_w2(),
            lambda: self._quant_w1(),
            lambda: self._weight_cast(),
            lambda: self.stats_ln("x0f", with_max=True, per_head=False,
                                  ccidx=1),
            lambda: self.mat1(ccidx=2),
            lambda: self.load_wq("w2q"),
            lambda: self.mat2("x0f", "x1f"),
            self.attn,
            lambda: self.load_wq("w1q"),
            lambda: self.stats_ln("x2f", with_max=True, per_head=False,
                                  ccidx=3),
            lambda: self.mat1(ccidx=4),
            lambda: self.load_wq("w2q"),
            lambda: self.mat2("x2f", "y"),
        ]
        self.const_init()
        for i, st in enumerate(steps):
            if i < upto:
                st()
        self.finish()



    def attn(self):
        self.stats_ln("x1f", with_max=False, per_head=True)
        self.load_wq("wwr")
        self.attn_all()

    def attn_all(self):
        nc, t = self.nc, self.t
        mmG, veV, scA, evS, stT = (self.mmG, self.veV, self.scA,
                                   self.evS, self.stT)
        PB = self.PB
        b = self.bases()

        vbl = [b["veV"]]
        for i in range(NTG):
            vbl.append(vbl[-1] + 61 + (1 if i % 4 == 0 else 0))
        vb = lambda tg: vbl[tg]
        eb = lambda tg: b["evS"] + 46 * tg
        mb = lambda tg: b["mmG"] + 28 * tg
        sb = lambda tg: b["stT"] + 65 * tg
        ab = lambda tg: b["scA"] + tg
        vstt0 = lambda tg: 21 + (1 if tg % 4 == 0 else 0)
        vstt = lambda tg, n, h: vb(tg) + vstt0(tg) + 4 * n + h + 1
        vres = lambda tg, dt: vb(tg) + vstt0(tg) + 32 + dt + 1
        sc_t = lambda tg, n, k: eb(tg) + 6 + 4 * n + k
        wr0 = lambda tg: eb(tg) + 38
        wkpm = lambda tg: t["WKPM"] if tg % 2 == 0 else t["WKPMB"]
        wkc_l = []
        c = 0
        for i in range(NTG):
            c += ND + (ND if i % 4 == 0 else 0)
            wkc_l.append(c)
        tp_T_end = lambda tg: 32 * (tg + 1) + 2 * tg
        tp_all = lambda tg: 34 * (tg + 1)

        def at_blk(h, n):
            a = (n // 2) % 2
            return t["ATB"][(n % 2) * 64:(n % 2) * 64 + 64,
                            (2 * h + n // 4) * 128 + a * 64:
                            (2 * h + n // 4) * 128 + a * 64 + 64]

        def grid(tn, h, n):
            a = (n // 2) % 2
            cb = (n % 2) + 2 * (2 * h + n // 4)
            return tn[a * 64:a * 64 + 64, cb * 64:cb * 64 + 64]

        def gpsum(which, h, n):
            a = (n // 2) % 2
            cb = (n % 2) + 2 * (2 * h + n // 4)
            bank = (0 if which == 0 else 2) + (0 if cb < 8 else 1)
            return PB[bank][a * 64:a * 64 + 64,
                            (cb % 8) * 64:(cb % 8) * 64 + 64], a

        with nc.Block() as blk:
            @blk.sync
            def _(e):
                self.entry_waits(e, b)

                def loads(tg):
                    for c_ in range(ND):
                        e.dma_start(out=self.rk_sl(c_),
                                    in_=bass.AP(t["rkbuf"],
                                                c_ * 128 * R + tg * TG,
                                                [[R, 128], [1, TG]])
                                    ).then_inc(self.dsl[2].h, 16)
                    for c_ in range(ND):
                        if tg % 4 == 0:
                            e.dma_start(out=self.wk_sl(c_)[:, 1:512],
                                        in_=bass.AP(t["rkbuf"],
                                                    c_ * 128 * R + tg * TG,
                                                    [[R, 128], [1, TG - 1]])
                                        ).then_inc(self.dsl[3].h, 16)
                            e.dma_start(out=self.wk_sl(c_)[:, 0:1],
                                        in_=t["ZCOL"][:, 0:1]
                                        ).then_inc(self.dsl[3].h, 16)
                        else:
                            e.dma_start(out=self.wk_sl(c_),
                                        in_=bass.AP(t["rkbuf"],
                                                    c_ * 128 * R + tg * TG - 1,
                                                    [[R, 128], [1, TG]])
                                        ).then_inc(self.dsl[3].h, 16)
                    for c_ in range(ND):
                        e.dma_start(out=self.ub_sl(c_),
                                    in_=bass.AP(t["ubbuf"],
                                                c_ * 128 * R + tg * TG,
                                                [[R, 128], [1, TG]])
                                    ).then_inc(self.dsb[0].h, 16)

                def transposes(tg):
                    e.wait_ge(self.dsl[3].h, b["dsl3"] + 16 * wkc_l[tg])
                    W = wkpm(tg)
                    for s in range(4):
                        for j in range(ND):
                            e.dma_start_transpose(
                                W[:, (s * 8 + j) * 128:(s * 8 + j + 1) * 128],
                                self.wk_sl(j)[:, s * 128:(s + 1) * 128]
                                ).then_inc(self.dtp[0].h, 16)

                loads(0)
                e.dma_start(out=t["AXA"][:, :],
                            in_=bass.AP(t["lnbuf"], 0,
                                        [[R, 128], [128 * R, ND], [1, TG]])
                            ).then_inc(self.dsl[0].h, 16)
                e.dma_start(out=t["AXB"][:, :],
                            in_=bass.AP(t["x1f"], 0,
                                        [[R, 128], [128 * R, ND], [1, TG]])
                            ).then_inc(self.dsl[1].h, 16)
                transposes(0)
                for tg in range(NTG):
                    e.wait_ge(scA.h, ab(tg) + 1)
                    e.dma_start(out=t["WKPMG2"][0:64, :],
                                in_=t["WKPMG"][64:128, :]
                                ).then_inc(self.dtp[0].h, 16)
                    e.dma_start(out=t["WKPMG2"][64:128, :],
                                in_=t["WKPMG"][0:64, :]
                                ).then_inc(self.dtp[0].h, 16)
                    if tg < NTG - 1:
                        e.wait_ge(mmG.h, mb(tg) + 10)
                        e.wait_ge(self.dtp[0].h, b["dtp0"] + 16 * tp_T_end(tg))
                        loads(tg + 1)
                        transposes(tg + 1)
                        e.wait_ge(veV.h, vres(tg, 7))
                        e.wait_ge(self.dstp[0].h,
                                  b["dstp0"] + 16 * 2 * (tg + 1))
                        e.dma_start(out=t["AXA"][:, :],
                                    in_=bass.AP(t["lnbuf"], (tg + 1) * TG,
                                                [[R, 128], [128 * R, ND],
                                                 [1, TG]])
                                    ).then_inc(self.dsl[0].h, 16)
                        e.dma_start(out=t["AXB"][:, :],
                                    in_=bass.AP(t["x1f"], (tg + 1) * TG,
                                                [[R, 128], [128 * R, ND],
                                                 [1, TG]])
                                    ).then_inc(self.dsl[1].h, 16)
                self.dsl[0].n += 16 * NTG
                self.dsl[1].n += 16 * NTG
                self.dsl[2].n += 16 * ND * NTG
                self.dsl[3].n += 16 * wkc_l[-1]
                self.dsb[0].n += 16 * ND * NTG
                self.dtp[0].n += 16 * tp_all(NTG - 1)

            @blk.gpsimd
            def _(e):
                for tg in range(NTG):
                    for hl in range(2):
                        e.wait_ge(veV.h, vres(tg, hl * 4 + 3))
                        e.dma_start(out=bass.AP(t["x2f"],
                                                hl * 4 * 128 * R + tg * TG,
                                                [[R, 128], [128 * R, 4],
                                                 [1, TG]]),
                                    in_=t["AXB"][:, hl * 4 * TG:
                                                 (hl * 4 + 4) * TG]
                                    ).then_inc(self.dstp[0].h, 16)
                self.dstp[0].n += 16 * 2 * NTG

            @blk.vector
            def _(e):
                for tg in range(NTG):
                    e.wait_ge(self.dsl[2].h, b["dsl2"] + 16 * ND * (tg + 1))
                    for h in range(H):
                        for j in range(2):
                            c_ = 2 * h + j
                            li = e.tensor_tensor(self.ga_sl(c_), self.rk_sl(c_),
                                                 self.GPAT[h][:, :], OP.mult)
                    li.then_inc(veV.h, 1)
                    for idx in range(8):
                        e.wait_ge(mmG.h, mb(tg) + idx + 1)
                        s, nh = divmod(idx, 2)
                        e.tensor_scalar_mul(
                            t["VPM"][:, s * 1024 + nh * 512:
                                     s * 1024 + nh * 512 + 512],
                            PB[2 + idx % 2][:, :], -1.0).then_inc(veV.h, 1)
                    e.wait_ge(mmG.h, mb(tg) + 9)
                    e.tensor_tensor(t["PSB"][:, 0:512], PB[0][:, :],
                                    t["NEGLS"][:, 0:512], OP.mult)
                    e.tensor_tensor(t["PSB"][:, 512:1024], PB[1][:, :],
                                    t["NEGLS"][:, 512:1024], OP.mult)
                    e.tensor_tensor(t["PTSB"][:, 0:512], PB[0][:, :],
                                    t["NEGLST"][:, 0:512], OP.mult)
                    li = e.tensor_tensor(t["PTSB"][:, 512:1024], PB[1][:, :],
                                         t["NEGLST"][:, 512:1024], OP.mult)
                    li.then_inc(veV.h, 1)
                    e.wait_ge(mmG.h, mb(tg) + 10)
                    e.tensor_tensor(t["INTRAT"][:, 0:512], PB[2][:, :],
                                    t["ALPHALMT"][:, 0:512], OP.mult)
                    e.tensor_tensor(t["INTRAT"][:, 512:1024], PB[3][:, :],
                                    t["ALPHALMT"][:, 512:1024], OP.mult
                                    ).then_inc(veV.h, 1)
                    li = e.tensor_tensor(t["XSB"][:, :], t["PSB"][:, :],
                                         t["EYEB"][:, :], OP.add)
                    li.then_inc(veV.h, 1)
                    for rnd in range(5):
                        if rnd < 4:
                            e.wait_ge(mmG.h, mb(tg) + 11 + 2 * rnd)
                            e.tensor_copy(t["QSB"][:, 0:512], PB[4][:, :])
                            li = e.tensor_copy(t["QTSB"][:, 0:512], PB[6][:, :])
                            li.then_inc(veV.h, 1)
                        e.wait_ge(mmG.h, mb(tg) + 12 + 2 * rnd)
                        e.tensor_copy(t["XSB"][:, 0:512], PB[2][:, :])
                        li = e.tensor_copy(t["XSB"][:, 512:1024], PB[3][:, :])
                        li.then_inc(veV.h, 1)
                    if tg % 4 == 0:
                        li = e.memset(t["SBFT"][:, :], 0.0)
                        li.then_inc(veV.h, 1)
                    for n in range(8):
                        for h in range(H):
                            e.wait_ge(stT.h, sb(tg) + 1 + 8 * n + 5 + h)
                            li = e.scalar_tensor_tensor(
                                t["SBFT"][:, h * 512:(h + 1) * 512],
                                t["SBFT"][:, h * 512:(h + 1) * 512],
                                t["GC128"][:, h:h + 1],
                                PB[4 + h][:, 0:512], OP.mult, OP.add)
                            li.then_inc(veV.h, 1)
                    e.wait_ge(self.dsl[0].h, b["dsl0"] + 16 * (tg + 1))
                    e.wait_ge(self.dsl[1].h, b["dsl1"] + 16 * (tg + 1))
                    for dt in range(ND):
                        e.wait_ge(evS.h, wr0(tg) + dt + 1)
                        e.tensor_add(t["AXB"][:, dt * TG:(dt + 1) * TG],
                                     t["AXB"][:, dt * TG:(dt + 1) * TG],
                                     t["AXA"][:, dt * TG:(dt + 1) * TG])
                        li = e.tensor_add(t["AXB"][:, dt * TG:(dt + 1) * TG],
                                          t["AXB"][:, dt * TG:(dt + 1) * TG],
                                          t["ATT"][:, :])
                        li.then_inc(veV.h, 1)
                veV.n = vb(NTG)

            @blk.scalar
            def _(e):
                for tg in range(NTG):
                    e.wait_ge(self.dtp[0].h, b["dtp0"] + 16 * tp_T_end(tg))
                    W = wkpm(tg)
                    for s in range(4):
                        for h in range(H):
                            base = (s * 8 + 2 * h) * 128
                            li = e.activation(t["WKPMG"][:, base:base + 256],
                                              W[:, base:base + 256],
                                              AF.Identity,
                                              scale=t["GT128"][:, h:h + 1])
                    li.then_inc(scA.h, 1)
                    for rnd in range(5):
                        e.wait_ge(mmG.h, mb(tg) + 11 + 2 * rnd)
                        if rnd < 4:
                            e.activation(t["QSB"][:, 512:1024], PB[5][:, :],
                                         AF.Identity)
                        else:
                            e.activation(t["QTSB"][:, 0:512], PB[6][:, :],
                                         AF.Identity)
                        li = e.activation(t["QTSB"][:, 512:1024], PB[7][:, :],
                                          AF.Identity)
                        li.then_inc(evS.h, 1)
                    e.wait_ge(stT.h, sb(tg) + 1)
                    for m in range(8):
                        li = e.activation(t["ATB"][:, m * 128:(m + 1) * 128],
                                          PB[m // 4][:, (m % 4) * 128:
                                                     (m % 4 + 1) * 128],
                                          AF.Identity)
                    li.then_inc(evS.h, 1)
                    for n in range(8):
                        a = (n // 2) % 2
                        e.wait_ge(stT.h, sb(tg) + 1 + 8 * n + 1)
                        e.activation(t["WKC"][:, :], PB[0][:, :], AF.Identity
                                     ).then_inc(evS.h, 1)
                        e.wait_ge(stT.h, sb(tg) + 1 + 8 * n + 2)
                        e.activation(t["VNB"][a * 64:a * 64 + 64, 0:512],
                                     PB[1][a * 64:a * 64 + 64, :], AF.Identity
                                     ).then_inc(evS.h, 1)
                        e.wait_ge(stT.h, sb(tg) + 1 + 8 * n + 3)
                        e.activation(t["VNB"][a * 64:a * 64 + 64, 512:1024],
                                     PB[2][a * 64:a * 64 + 64, :], AF.Identity
                                     ).then_inc(evS.h, 1)
                        e.wait_ge(stT.h, sb(tg) + 1 + 8 * n + 4)
                        for b2 in range(8):
                            li = e.activation(
                                t["OSB"][:, b2 * 512 + n * 64:
                                         b2 * 512 + n * 64 + 64],
                                PB[3][:, b2 * 64:(b2 + 1) * 64], AF.Identity)
                        li.then_inc(evS.h, 1)
                    for dt in range(ND):
                        e.wait_ge(mmG.h, mb(tg) + 20 + dt + 1)
                        if dt > 0:
                            e.wait_ge(veV.h, vres(tg, dt - 1))
                        e.activation(t["ATT"][:, :], PB[4 + dt % 4][:, :],
                                     AF.Identity).then_inc(evS.h, 1)
                scA.n = ab(NTG)
                evS.n = eb(NTG)

            @blk.tensor
            def _(e):
                e.wait_ge(self.sqL.h, b["sqL"])
                for tg in range(NTG):
                    W = wkpm(tg)
                    e.wait_ge(self.dsb[0].h, b["dsb0"] + 16 * ND * (tg + 1))
                    if tg > 0:
                        e.wait_ge(evS.h, eb(tg) - 8)
                    for idx in range(8):
                        s, nh = divmod(idx, 2)
                        psum = PB[2 + idx % 2]
                        if idx >= 2:
                            e.wait_ge(veV.h, vb(tg) + idx)
                        for k in range(ND):
                            mi = e.matmul(psum[:, :],
                                          self.ub_sl(k)[:, s * 128:(s + 1) * 128],
                                          self.wbuf[k][:, nh * 512:(nh + 1) * 512],
                                          start=(k == 0), stop=(k == ND - 1))
                        mi.then_inc(mmG.h, 1)
                    e.wait_ge(self.dsl[3].h, b["dsl3"] + 16 * wkc_l[tg])
                    e.wait_ge(self.dsl[2].h, b["dsl2"] + 16 * ND * (tg + 1))
                    for h in range(H):
                        for n in range(8):
                            blkp, a = gpsum(0, h, n)
                            for jj in range(2):
                                c_ = 2 * h + jj
                                mi = e.matmul(blkp,
                                              self.wk_sl(c_)[:, n * 64:(n + 1) * 64],
                                              self.wk_sl(c_)[:, n * 64:(n + 1) * 64],
                                              start=(jj == 0), stop=(jj == 1),
                                              tile_position=(0, a * 64))
                    mi.then_inc(mmG.h, 1)
                    e.wait_ge(veV.h, vb(tg) + 9)
                    for h in range(H):
                        for n in range(8):
                            blkp, a = gpsum(1, h, n)
                            for jj in range(2):
                                c_ = 2 * h + jj
                                mi = e.matmul(blkp,
                                              self.wk_sl(c_)[:, n * 64:(n + 1) * 64],
                                              self.rk_sl(c_)[:, n * 64:(n + 1) * 64],
                                              start=(jj == 0), stop=(jj == 1),
                                              tile_position=(0, a * 64))
                    mi.then_inc(mmG.h, 1)
                    e.wait_ge(veV.h, vb(tg) + 12)
                    if tg > 0:
                        e.wait_ge(veV.h, vb(tg) - 8)
                    for h in range(H):
                        for n in range(8):
                            a = (n // 2) % 2
                            cb = (n % 2) + 2 * (2 * h + n // 4)
                            px = PB[2 if cb < 8 else 3][a * 64:a * 64 + 64,
                                                        (cb % 8) * 64:
                                                        (cb % 8) * 64 + 64]
                            mi = e.matmul(px,
                                          t["EYE128"][a * 64:a * 64 + 64, 0:64],
                                          grid(t["XSB"], h, n), start=True,
                                          stop=False,
                                          tile_position=(a * 64, a * 64))
                    qsrc, qtsrc = "PSB", "PTSB"
                    e.wait_ge(evS.h, eb(tg))
                    for rnd in range(5):
                        if rnd >= 1:
                            e.wait_ge(veV.h, vb(tg) + 11 + 2 * rnd)
                            e.wait_ge(evS.h, eb(tg) + rnd)
                        for h in range(H):
                            for n in range(8):
                                a = (n // 2) % 2
                                cb = (n % 2) + 2 * (2 * h + n // 4)
                                if rnd < 4:
                                    pq = PB[4 if cb < 8 else 5][
                                        a * 64:a * 64 + 64,
                                        (cb % 8) * 64:(cb % 8) * 64 + 64]
                                    e.matmul(pq, grid(t[qtsrc], h, n),
                                             grid(t[qsrc], h, n),
                                             start=True, stop=True,
                                             tile_position=(a * 64, a * 64))
                                pqt = PB[6 if cb < 8 else 7][
                                    a * 64:a * 64 + 64,
                                    (cb % 8) * 64:(cb % 8) * 64 + 64]
                                mi = e.matmul(pqt, grid(t[qsrc], h, n),
                                              grid(t[qtsrc], h, n),
                                              start=True, stop=True,
                                              tile_position=(a * 64, a * 64))
                        mi.then_inc(mmG.h, 1)
                        qsrc, qtsrc = "QSB", "QTSB"
                        e.wait_ge(evS.h, eb(tg) + rnd + 1)
                        if rnd < 4:
                            e.wait_ge(veV.h, vb(tg) + 13 + 2 * rnd)
                        if rnd >= 1:
                            e.wait_ge(veV.h, vb(tg) + 12 + 2 * rnd)
                        for h in range(H):
                            for n in range(8):
                                a = (n // 2) % 2
                                cb = (n % 2) + 2 * (2 * h + n // 4)
                                px = PB[2 if cb < 8 else 3][
                                    a * 64:a * 64 + 64,
                                    (cb % 8) * 64:(cb % 8) * 64 + 64]
                                mi = e.matmul(px, grid(t["QTSB"], h, n),
                                              grid(t["XSB"], h, n),
                                              start=False, stop=(rnd == 4),
                                              tile_position=(a * 64, a * 64))
                        mi.then_inc(mmG.h, 1)
                    e.wait_ge(veV.h, vb(tg) + 21)
                    for m in range(8):
                        mi = e.matmul(PB[m // 4][:, (m % 4) * 128:
                                                 (m % 4 + 1) * 128],
                                      t["XSB"][:, m * 128:(m + 1) * 128],
                                      t["EYEF"][:, :], start=True, stop=True)
                    mi.then_inc(stT.h, 1)
                    e.wait_ge(self.dtp[0].h, b["dtp0"] + 16 * tp_all(tg))
                    for n in range(8):
                        a = (n // 2) % 2
                        p0 = (n % 2) * 64
                        sbase = (n // 2) * 8
                        if n > 0:
                            e.wait_ge(evS.h, sc_t(tg, n - 1, 1))
                        else:
                            e.wait_ge(evS.h, eb(tg) + 6)
                        for h in range(H):
                            for cs in range(2):
                                base = (sbase + 2 * h) * 128
                                mi = e.matmul(
                                    PB[0][:, (2 * h + cs) * 64:
                                          (2 * h + cs) * 64 + 64],
                                    W[p0:p0 + 64, base + cs * 128:
                                      base + (cs + 1) * 128],
                                    at_blk(h, n), start=True, stop=True,
                                    tile_position=(p0, 0))
                        mi.then_inc(stT.h, 1)
                        e.wait_ge(evS.h, sc_t(tg, n, 1))
                        if n > 0:
                            e.wait_ge(evS.h, sc_t(tg, n - 1, 3))
                        else:
                            e.wait_ge(veV.h, vb(tg) + vstt0(tg))
                        for h in range(H):
                            if n > 0:
                                e.wait_ge(veV.h, vstt(tg, n - 1, h))
                            psum = PB[1] if h < 2 else PB[2]
                            col = (h % 2) * 256
                            for cs in range(2):
                                e.matmul(psum[a * 64:a * 64 + 64, col:col + 256],
                                         t["WKC"][:, (2 * h + cs) * 64:
                                                  (2 * h + cs) * 64 + 64],
                                         t["SBFT"][:, (2 * h + cs) * 256:
                                                   (2 * h + cs) * 256 + 256],
                                         start=(cs == 0), stop=False,
                                         tile_position=(0, a * 64))
                            mi = e.matmul(psum[a * 64:a * 64 + 64, col:col + 256],
                                          at_blk(h, n),
                                          t["VPM"][p0:p0 + 64,
                                                   (n // 2) * 1024 + h * 256:
                                                   (n // 2) * 1024 + h * 256 + 256],
                                          start=False, stop=True,
                                          tile_position=(p0, a * 64))
                            if h == 1 or h == 3:
                                mi.then_inc(stT.h, 1)
                        e.wait_ge(evS.h, sc_t(tg, n, 3))
                        if n > 0:
                            e.wait_ge(evS.h, sc_t(tg, n - 1, 4))
                        for h in range(H):
                            for cs in range(2):
                                ob = PB[3][:, (2 * h + cs) * 64:
                                           (2 * h + cs) * 64 + 64]
                                for es in range(2):
                                    c_ = 2 * h + es
                                    e.matmul(ob,
                                             t["SBFT"][:, c_ * 256 + cs * 128:
                                                       c_ * 256 + (cs + 1) * 128],
                                             self.ga_sl(c_)[:, n * 64:(n + 1) * 64],
                                             start=(es == 0), stop=False)
                                mi = e.matmul(ob,
                                              t["VNB"][a * 64:a * 64 + 64,
                                                       h * 256 + cs * 128:
                                                       h * 256 + (cs + 1) * 128],
                                              grid(t["INTRAT"], h, n),
                                              start=False, stop=True,
                                              tile_position=(a * 64, 0))
                        mi.then_inc(stT.h, 1)
                        use_orig = (n % 2) == a
                        G = t["WKPMG"] if use_orig else t["WKPMG2"]
                        for h in range(H):
                            base = (sbase + 2 * h) * 128
                            for es in range(2):
                                mi = e.matmul(
                                    PB[4 + h][:, es * 256:(es + 1) * 256],
                                    G[a * 64:a * 64 + 64,
                                      base + es * 128:base + (es + 1) * 128],
                                    t["VNB"][a * 64:a * 64 + 64,
                                             h * 256:h * 256 + 256],
                                    start=True, stop=True,
                                    tile_position=(a * 64, 0))
                            mi.then_inc(stT.h, 1)
                    e.wait_ge(evS.h, wr0(tg))
                    for dt in range(ND):
                        if dt < 4:
                            e.wait_ge(veV.h, vstt(tg, 7, dt))
                        else:
                            e.wait_ge(evS.h, wr0(tg) + (dt - 4) + 1)
                        for k in range(ND):
                            mi = e.matmul(PB[4 + dt % 4][:, :],
                                          self.wbuf[k][:, 1024 + dt * 128:
                                                       1024 + (dt + 1) * 128],
                                          t["OSB"][:, k * TG:(k + 1) * TG],
                                          start=(k == 0), stop=(k == ND - 1))
                        mi.then_inc(mmG.h, 1)
                stT.n = sb(NTG)
                mmG.n = mb(NTG)

def make_constants():
    i = np.arange(C)
    diffs = np.maximum(i[:, None] - i[None, :], 0).astype(np.float32)
    strict = (i[:, None] > i[None, :]).astype(np.float32)
    eye = np.eye(C, dtype=np.float32)
    tile = lambda m: np.tile(m, (2, 16)).astype(np.float32)
    return {
        "cdiffsg": tile(diffs),
        "cnstrictg": tile(-strict),
        "cdiffstg": tile(diffs.T),
        "cnstricttg": tile(-strict.T),
        "cstricttg": tile(strict.T),
        "ceyeb": tile(eye),
        "ceye128": np.tile(eye, (2, 1)).astype(np.float32),
    "ceyef": np.eye(128, dtype=np.float32),
        "ciota": i[None, :].astype(np.float32),
        "criota": i[::-1][None, :].astype(np.float32).copy(),
    }


def shard_inputs(L0_attn_out, L0_attn_Ww, L0_attn_Wr, L0_attn_decay,
                 L0_attn_log_alpha, L0_ffn_w1, L0_ffn_b1, L0_ffn_w2, L0_ffn_b2):
    cn = make_constants()
    common = {
        "w1": np.ascontiguousarray(np.asarray(L0_ffn_w1, np.float32)),
        "w2": np.ascontiguousarray(np.asarray(L0_ffn_w2, np.float32)),
        "wwt": np.ascontiguousarray(np.asarray(L0_attn_Ww, np.float32).T),
        "wrt": np.ascontiguousarray(np.asarray(L0_attn_Wr, np.float32).T),
        "b1r": np.ascontiguousarray(np.asarray(L0_ffn_b1, np.float32).reshape(32, 128).T),
        "b2r": np.ascontiguousarray(np.asarray(L0_ffn_b2, np.float32).reshape(8, 128).T),
        "decay": np.asarray(L0_attn_decay, np.float32).reshape(1, H),
        "logalpha": np.asarray(L0_attn_log_alpha, np.float32).reshape(1, H),
    }
    common.update(cn)
    x = np.asarray(L0_attn_out, np.float32)
    in_maps = []
    for c in range(8):
        m = dict(common)
        m["x0f"] = np.ascontiguousarray(x[2 * c:2 * c + 2].reshape(R, D).T)
        in_maps.append(m)
    return in_maps


def build_program(dump=(), upto=99, races=True):
    nc = bass.Bass("TRN2", target_bir_lowering=False,
                   detect_race_conditions=races)
    b = Builder(nc, dump=dump)
    b.build(upto=upto)
    return nc


def kernel(**inputs):
    in_maps = shard_inputs(**inputs)
    nc = build_program(DUMP)
    res = run_bass_kernel_spmd(nc, in_maps, core_ids=list(range(8)))
    out = np.empty((16, 2048, 1024), np.float32)
    for c in range(8):
        out[2 * c:2 * c + 2] = res.results[c]["y"].T.reshape(2, 2048, 1024)
    return out


# version 31
# speedup vs baseline: 1.0354x; 1.0354x over previous
import numpy as np
import ml_dtypes
import concourse.bass as bass
import concourse.mybir as mybir
from concourse.bass_utils import run_bass_kernel_spmd

F32 = mybir.dt.float32
BF16 = mybir.dt.bfloat16
U8 = mybir.dt.uint8
AF = mybir.ActivationFunctionType
OP = mybir.AluOpType
AX = mybir.AxisListType.X

MAGIC = 12582912.0
RRLIM = 4194304.0
NEGBIG = -3.0e38
D = 1024
ND = 8
R = 4096
FF = 4096
NF = 32
H = 4
HD = 256
C = 64
TG = 512
NTG = 8
EPSQ = 1e-8
EPSLN = 1e-5

DUMP = ()

SLAB_MAP = {
    "CG0": (0, [128, 1024], F32),
    "CG1": (4096, [128, 1024], F32),
    "WT0": (0, [128, 4096], F32),
    "WT1": (16384, [128, 4096], F32),
    "IBC": (32768, [128, 4096], F32),
    "TQAp": (49152, [128, 4096], F32),
    "TQBp": (65536, [128, 4096], F32),
    "WCST": (49152, [128, 4096], BF16),
    "W2T1": (81920, [128, 1024], BF16),
    "W2T2": (83968, [128, 1024], BF16),
    "XA0": (0, [128, 4096], F32),
    "XA1": (16384, [128, 4096], F32),
    "LNT0": (32768, [128, 4096], F32),
    "XSQ": (49152, [128, 4096], BF16),
    "XBF": (57344, [128, 4096], BF16),
    "RB0": (65536, [128, 512], F32),
    "MRB0": (67584, [128, 512], F32),
    "RB1": (69632, [128, 512], F32),
    "MRB1": (71680, [128, 512], F32),
    "MB0": (73728, [128, 512], F32),
    "MB1": (75776, [128, 512], F32),
    "VB0": (77824, [128, 512], F32),
    "VB1": (79872, [128, 512], F32),
    "TB0": (81920, [128, 512], F32),
    "TB1": (83968, [128, 512], F32),
    "SH0": (86016, [128, 512], F32),
    "RN0": (88064, [128, 512], F32),
    "RN1": (90112, [128, 512], F32),
    "RN2": (92160, [128, 512], F32),
    "RN3": (94208, [128, 512], F32),
    "RNALL": (88064, [128, 2048], F32),
    "GSC": (104448, [128, 2048], F32),
    "UBT": (96256, [128, 4096], BF16),
    "MA0": (0, [128, 4096], F32),
    "MA1": (16384, [128, 4096], F32),
    "MTQ": (32768, [128, 4096], F32),
    "QX0": (49152, [128, 4096], BF16),
    "QX1": (57344, [128, 4096], BF16),
    "HT0": (65536, [128, 512], F32),
    "HT1": (67584, [128, 512], F32),
    "HT2": (69632, [128, 512], F32),
    "HT3": (71680, [128, 512], F32),
    "HTQ": (0, [128, 2048], F32),
    "ATQ": (8192, [128, 2048], F32),
    "XR0": (16384, [128, 2048], F32),
    "XR1": (24576, [128, 2048], F32),
    "XO0": (32768, [128, 2048], F32),
    "XO1": (40960, [128, 2048], F32),
    "YT0": (49152, [128, 256], F32),
    "YT1": (50176, [128, 256], F32),
    "YT2": (51200, [128, 256], F32),
    "YT3": (52224, [128, 256], F32),
    "HTQ4": (86016, [128, 4096], F32),
    "QH0": (53248, [128, 8192], BF16),
    "QH1": (69632, [128, 8192], BF16),
    "AXA": (0, [128, 4096], F32),
    "AXB": (16384, [128, 4096], F32),
    "WKPMB": (32768, [128, 4096], BF16),
    "ATT": (49152, [128, 512], F32),
    "VPM": (51200, [128, 4096], BF16),
    "WKPM": (59392, [128, 4096], BF16),
    "WKPMG": (67584, [128, 4096], BF16),
    "WKPMG2": (75776, [128, 4096], BF16),
    "PSB": (83968, [128, 1024], BF16),
    "PTSB": (86016, [128, 1024], BF16),
    "INTRAT": (88064, [128, 1024], BF16),
    "XSB": (90112, [128, 1024], BF16),
    "QSB": (92160, [128, 1024], BF16),
    "QTSB": (94208, [128, 1024], BF16),
    "ATB": (96256, [128, 1024], BF16),
    "WKC": (98304, [128, 512], BF16),
    "VNB": (99328, [128, 1024], BF16),
    "SBFT": (101376, [128, 2048], BF16),
    "OSB": (105472, [128, 4096], BF16),
}
SLAB_BYTES = 113664


def ap4(tensor, off):
    return bass.AP(tensor, off, [[32, 4], [1, 512]])


class Sem:
    def __init__(self, nc, name):
        self._cm = nc.semaphore(name)
        self.h = self._cm.__enter__()
        self.n = 0


class Builder:
    def __init__(self, nc, dump=(), extern=()):
        self.nc = nc
        self.dump = dump
        self.extern = extern
        self.t = {}

    def dram(self, name, shape, dtype=F32, kind="Internal"):
        if kind == "Internal" and name in self.dump:
            kind = "ExternalOutput"
        if kind == "Internal" and name in self.extern:
            kind = "ExternalInput"
        h = self.nc.dram_tensor(name, shape, dtype, kind=kind)
        self.t[name] = h
        return h

    def sb(self, name, shape, dtype):
        cm = self.nc.sbuf_tensor(name, shape, dtype)
        h = cm.__enter__()
        self.t[name] = h
        return h

    def derive_q(self, e, swname, ntile, ssw, bsrc, bq, with_half):
        t = self.t
        self.fv(e, e.tensor_scalar_mul(t[ssw][:, 0:ntile], t[swname][:, 0:ntile],
                                       t["S1C"][:, 0:1]))
        self.fv(e, e.reciprocal(t["QT5"][:, 0:ntile], t[ssw][:, 0:ntile]))
        self.fv(e, e.tensor_tensor(t["QT6"][:, 0:ntile], t[bsrc][:, 0:ntile],
                                   t["QT5"][:, 0:ntile], OP.mult))
        self.fv(e, e.tensor_scalar_add(t["QT7"][:, 0:ntile], t["QT6"][:, 0:ntile],
                                       MAGIC))
        self.fv(e, e.tensor_scalar_add(t["QT7"][:, 0:ntile], t["QT7"][:, 0:ntile],
                                       -MAGIC))
        self.fv(e, e.tensor_scalar_mul(t["QT8"][:, 0:ntile], t["QT6"][:, 0:ntile],
                                       -1.0))
        self.fv(e, e.tensor_max(t["QT8"][:, 0:ntile], t["QT8"][:, 0:ntile],
                                t["QT6"][:, 0:ntile]))
        self.fv(e, e.tensor_single_scalar(t["QTM"][:, 0:ntile],
                                          t["QT8"][:, 0:ntile], RRLIM, OP.is_lt))
        self.fv(e, e.select(t["QT6"][:, 0:ntile], t["QTM"][:, 0:ntile],
                            t["QT7"][:, 0:ntile], t["QT6"][:, 0:ntile]))
        self.fv(e, e.tensor_tensor(t[bq][:, 0:ntile], t["QT6"][:, 0:ntile],
                                   t[ssw][:, 0:ntile], OP.mult))
        if with_half:
            self.fv(e, e.tensor_scalar_mul(t[ssw][:, 0:ntile], t[ssw][:, 0:ntile],
                                           0.5))
            self.fv(e, e.tensor_scalar_mul(t[bq][:, 0:ntile], t[bq][:, 0:ntile],
                                           0.5))

    def entry_waits(self, e, b):
        for nm, s_ in (("gqS", self.gqS), ("dln", self.dln), ("drn", self.drn),
                       ("dstp0", self.dstp[0]), ("dstp1", self.dstp[1]),
                       ("hts0", self.hts[0]), ("hts1", self.hts[1]),
                       ("hts2", self.hts[2]), ("hts3", self.hts[3])):
            if b[nm] > 0:
                e.wait_ge(s_.h, b[nm])

    def fv(self, e, instr):
        self.ssV.n += 1
        instr.then_inc(self.ssV.h, 1)
        e.wait_ge(self.ssV.h, self.ssV.n)

    def fa(self, e, instr):
        self.ssA.n += 1
        instr.then_inc(self.ssA.h, 1)
        e.wait_ge(self.ssA.h, self.ssA.n)

    def rk_sl(self, c):
        return self.wbuf[c][:, 2048:2560]

    def wk_sl(self, c):
        return self.wbuf[c][:, 2560:3072]

    def ub_sl(self, c):
        return self.wbuf[c][:, 3072:3584]

    def ga_sl(self, c):
        return self.wbuf[c][:, 3584:4096]

    def declare(self):
        d = self.dram
        d("x0f", [D, R], kind="ExternalInput")
        d("w1", [FF, D], kind="ExternalInput")
        d("w2", [D, FF], kind="ExternalInput")
        d("wwt", [D, D], kind="ExternalInput")
        d("wrt", [D, D], kind="ExternalInput")
        d("b1r", [128, 32], kind="ExternalInput")
        d("b2r", [128, 8], kind="ExternalInput")
        d("decay", [1, H], kind="ExternalInput")
        d("logalpha", [1, H], kind="ExternalInput")
        d("cdiffsg", [128, 1024], kind="ExternalInput")
        d("cnstrictg", [128, 1024], kind="ExternalInput")
        d("cdiffstg", [128, 1024], kind="ExternalInput")
        d("cnstricttg", [128, 1024], kind="ExternalInput")
        d("cstricttg", [128, 1024], kind="ExternalInput")
        d("ceyeb", [128, 1024], kind="ExternalInput")
        d("ceye128", [128, 64], kind="ExternalInput")
        d("ceyef", [128, 128], kind="ExternalInput")
        d("ciota", [1, 64], kind="ExternalInput")
        d("criota", [1, 64], kind="ExternalInput")
        d("y", [D, R], kind="ExternalOutput")
        d("x1f", [D, R])
        d("x2f", [D, R])
        d("lnbuf", [D, R])
        d("hbuf", [FF, R])
        d("w1q", [D, FF], BF16)
        d("rkbuf", [D, R], BF16)
        d("ubbuf", [D, R], BF16)
        d("w2q", [FF, D], BF16)
        d("wwtb", [D, D], BF16)
        d("wrtb", [D, D], BF16)
        d("strow", [8, R])
        d("mrow", [1, 128])
        d("misc", [1, 64])
        d("misc2", [1, 64])
        d("gparow", [1, H * 512])
        d("gptrow", [1, 256])
        for i in (1, 2, 3, 4):
            d("cci%d" % i, [1, 64])
            d("cco%d" % i, [1, 64])

        s = self.sb
        self.wbuf = [s("wb%d" % i, [128, 4096], BF16) for i in range(ND)]
        s("ONESB", [128, 1], BF16)
        s("ONES128", [128, 128], BF16)
        s("EYE128", [128, 64], BF16)
        s("EYEF", [128, 128], BF16)
        s("RKTAIL", [128, 8], BF16)
        s("NEGLS", [128, 1024], F32)
        s("NEGLST", [128, 1024], F32)
        s("ALPHALMT", [128, 1024], F32)
        s("EYEB", [128, 1024], F32)
        self.GPAT = [s("GPAT%d" % h, [128, 512], F32) for h in range(H)]
        s("LGB", [128, 4], F32)
        s("ALB", [128, 4], F32)
        s("GC128", [128, 4], F32)
        s("GT128", [128, 4], F32)
        s("S1C", [128, 1], F32)
        s("IS1C", [128, 1], F32)
        s("RMAX", [128, 1], F32)
        s("HMAX", [128, 1], F32)
        s("TMPC", [128, 1], F32)
        s("SWV1", [128, 32], F32)
        s("SWV2", [128, 8], F32)
        s("B1S", [128, 32], F32)
        s("B2S", [128, 8], F32)
        s("S1SW", [128, 32], F32)
        s("B1Q", [128, 32], F32)
        s("S2SW", [128, 8], F32)
        s("B2Q", [128, 8], F32)
        s("QT5", [128, 32], F32)
        s("QT6", [128, 32], F32)
        s("QT7", [128, 32], F32)
        s("QT8", [128, 32], F32)
        s("QTM", [128, 32], U8)
        s("MROWS", [1, 128], F32)
        s("CCS", [1, 64], F32)
        s("SCC", [1, 64], F32)
        s("SCC128", [128, 1], F32)
        s("TMPG", [128, 1], F32)
        s("DRO", [1, 64], F32)
        s("GAMR", [1, 64], F32)
        s("LGR", [1, 64], F32)
        s("ALR", [1, 64], F32)
        s("GCR", [1, 64], F32)
        s("IOTAS", [1, 64], F32)
        s("RIOTAS", [1, 64], F32)
        s("GPAR", [1, 256], F32)
        s("GPTR", [1, 256], F32)

        self.const_vals = [MAGIC, -MAGIC, 1.0 / 127.0, 1.0 / 255.0, 1.0 / D,
                           0.5, EPSLN, EPSQ, 1e-30, 1e-12, -1.0, RRLIM, 64.0,
                           NEGBIG, 1.5, 2.0]
        cb = s("CONSTS", [128, len(self.const_vals)], F32)
        for i, v in enumerate(self.const_vals):
            self.nc.const_aps.register(F32, v, cb[:, i:i + 1])
        s("ZCOL", [128, 1], BF16)
        cbb = s("CONSTB", [128, 4], BF16)
        for i, v in enumerate([1.0, -1.0, 0.0, 0.5]):
            self.nc.const_aps.register(BF16, v, cbb[:, i:i + 1])
        self.const_bf_vals = [1.0, -1.0, 0.0, 0.5]

        slab = s("SLAB", [128, SLAB_BYTES], U8)
        base = self.nc.lookup_mloc(slab).addr
        for nm, (off, shape, dt_) in SLAB_MAP.items():
            self.t[nm] = self.nc.alloc_sbuf_tensor_at(nm, shape, dt_,
                                                      offset=base + off)

        self.PB = []
        self.PBb = []
        for i in range(8):
            cm = self.nc.psum_tensor("PB%d" % i, [128, 512], F32)
            self.PB.append(cm.__enter__())
            self.t["PB%d" % i] = self.PB[i]
            vb = self.nc.virtual_psum_tensor("PBb%d" % i, [128, 1024], BF16)
            self.nc.back_virtual_tensor_with(vb, self.PB[i])
            self.PBb.append(vb)

        for nm in ("sqL", "sqT", "gqS", "gqB", "wqL", "mmG", "veV", "scA",
                   "ccS", "evS", "evV", "stT", "snV", "snS", "snT", "snG",
                   "dln", "drn", "ssV", "ssA"):
            setattr(self, nm, Sem(self.nc, nm))
        self.dsl = [Sem(self.nc, "dsl%d" % i) for i in range(4)]
        self.dsb = [Sem(self.nc, "dsb%d" % i) for i in range(2)]
        self.dstp = [Sem(self.nc, "dstp%d" % i) for i in range(2)]
        self.dtp = [Sem(self.nc, "dtp%d" % i) for i in range(2)]
        self.hts = [Sem(self.nc, "hts%d" % i) for i in range(4)]

    def bases(self):
        d = {nm: getattr(self, nm).n for nm in
             ("sqL", "sqT", "gqS", "gqB", "wqL", "mmG", "veV", "scA",
              "ccS", "evS", "evV", "stT", "snV", "snS", "snT", "snG",
              "dln", "drn")}
        for i, s in enumerate(self.dsl):
            d["dsl%d" % i] = s.n
        for i, s in enumerate(self.dsb):
            d["dsb%d" % i] = s.n
        for i, s in enumerate(self.dstp):
            d["dstp%d" % i] = s.n
        for i, s in enumerate(self.dtp):
            d["dtp%d" % i] = s.n
        for i, s in enumerate(self.hts):
            d["hts%d" % i] = s.n
        return d

    def const_init(self):
        nc, t = self.nc, self.t
        with nc.Block() as blk:
            @blk.vector
            def _(e):
                for i, v in enumerate(self.const_vals):
                    e.memset(t["CONSTS"][:, i:i + 1], v)
                for i, v in enumerate(self.const_bf_vals):
                    e.memset(t["CONSTB"][:, i:i + 1], v)
                e.memset(t["ZCOL"][:, :], 0.0)

    def p0_scalars(self):
        nc, t = self.nc, self.t
        sqL, gqS, gqB, scA = self.sqL, self.gqS, self.gqB, self.scA
        b = self.bases()
        with nc.Block() as blk:
            @blk.sync
            def _(e):
                loads = [
                    (t["B1S"][:, :], t["b1r"].ap()),
                    (t["B2S"][:, :], t["b2r"].ap()),
                    (t["DRO"][0:1, 0:4], t["decay"].ap()),
                    (t["DRO"][0:1, 8:12], t["logalpha"].ap()),
                    (t["IOTAS"][:, :], t["ciota"].ap()),
                    (t["RIOTAS"][:, :], t["criota"].ap()),
                    (t["EYEB"][:, :], t["ceyeb"].ap()),
                ]
                for o, i in loads:
                    e.dma_start(out=o, in_=i).then_inc(sqL.h, 16)
                sqL.n = b["sqL"] + 16 * 7

            @blk.vector
            def _(e):
                e.memset(t["ONESB"][:, :], 1.0)
                e.memset(t["ONES128"][:, :], 1.0)
                e.memset(t["RMAX"][:, :], NEGBIG)
                e.memset(t["HMAX"][:, :], NEGBIG)

            @blk.scalar
            def _(e):
                e.wait_ge(sqL.h, b["sqL"] + 16 * 7)
                self.fa(e, e.activation(t["GAMR"][0:1, 0:4], t["DRO"][0:1, 0:4],
                                        AF.Sigmoid))
                self.fa(e, e.activation(t["LGR"][0:1, 0:4], t["GAMR"][0:1, 0:4],
                                        AF.Ln))
                e.activation(t["ALR"][0:1, 0:4], t["DRO"][0:1, 8:12], AF.Exp)
                e.activation(t["GCR"][0:1, 0:4], t["LGR"][0:1, 0:4], AF.Exp,
                             scale=64.0)
                for h in range(H):
                    e.activation(t["GPAR"][0:1, h * 64:(h + 1) * 64],
                                 t["IOTAS"][0:1, :], AF.Exp,
                                 scale=t["LGR"][0:1, h:h + 1],
                                 bias=t["DRO"][0:1, 8 + h:9 + h])
                    last = e.activation(t["GPTR"][0:1, h * 64:(h + 1) * 64],
                                        t["RIOTAS"][0:1, :], AF.Exp,
                                        scale=t["LGR"][0:1, h:h + 1])
                last.then_inc(scA.h, 1)
                scA.n = b["scA"] + 1

            @blk.gpsimd
            def _(e):
                e.wait_ge(scA.h, b["scA"] + 1)
                e.dma_start(out=bass.AP(t["misc"], 0, [[64, 1], [1, 4]]),
                            in_=t["LGR"][0:1, 0:4]).then_inc(gqS.h, 16)
                e.dma_start(out=bass.AP(t["misc"], 4, [[64, 1], [1, 4]]),
                            in_=t["ALR"][0:1, 0:4]).then_inc(gqS.h, 16)
                e.dma_start(out=bass.AP(t["misc"], 8, [[64, 1], [1, 4]]),
                            in_=t["GCR"][0:1, 0:4]).then_inc(gqS.h, 16)
                for h in range(H):
                    for j in range(8):
                        e.dma_start(out=bass.AP(t["gparow"], h * 512 + j * 64,
                                                [[2048, 1], [1, 64]]),
                                    in_=t["GPAR"][0:1, h * 64:(h + 1) * 64]
                                    ).then_inc(gqS.h, 16)
                e.dma_start(out=t["gptrow"].ap(), in_=t["GPTR"][0:1, :]
                            ).then_inc(gqS.h, 16)
                gqS.n = b["gqS"] + 16 * 36
                e.wait_ge(gqS.h, gqS.n)
                e.dma_start(out=t["LGB"][:, :],
                            in_=bass.AP(t["misc"], 0, [[0, 128], [1, 4]])
                            ).then_inc(gqB.h, 16)
                e.dma_start(out=t["ALB"][:, :],
                            in_=bass.AP(t["misc"], 4, [[0, 128], [1, 4]])
                            ).then_inc(gqB.h, 16)
                e.dma_start(out=t["GC128"][:, :],
                            in_=bass.AP(t["misc"], 8, [[0, 128], [1, 4]])
                            ).then_inc(gqB.h, 16)
                for h in range(H):
                    e.dma_start(out=self.GPAT[h][:, :],
                                in_=bass.AP(t["gparow"], h * 512,
                                            [[0, 128], [1, 512]])
                                ).then_inc(gqB.h, 16)
                for h in range(H):
                    e.dma_start(out=t["GT128"][0:64, h:h + 1],
                                in_=bass.AP(t["gptrow"], h * 64, [[1, 64], [1, 1]])
                                ).then_inc(gqB.h, 16)
                    e.dma_start(out=t["GT128"][64:128, h:h + 1],
                                in_=bass.AP(t["gptrow"], h * 64, [[1, 64], [1, 1]])
                                ).then_inc(gqB.h, 16)
                gqB.n = b["gqB"] + 16 * 15

    def p0_masks(self):
        nc, t = self.nc, self.t
        sqL, scA, veV = self.sqL, self.scA, self.veV
        b = self.bases()
        with nc.Block() as blk:
            @blk.sync
            def _(e):
                e.dma_start(out=t["CG0"][:, :], in_=t["cdiffsg"].ap()
                            ).then_inc(sqL.h, 16)
                e.wait_ge(scA.h, b["scA"] + 4)
                e.dma_start(out=t["CG0"][:, :], in_=t["cnstrictg"].ap()
                            ).then_inc(sqL.h, 16)
                e.wait_ge(veV.h, b["veV"] + 1)
                e.dma_start(out=t["CG0"][:, :], in_=t["cdiffstg"].ap()
                            ).then_inc(sqL.h, 16)
                e.wait_ge(scA.h, b["scA"] + 8)
                e.dma_start(out=t["CG0"][:, :], in_=t["cnstricttg"].ap()
                            ).then_inc(sqL.h, 16)
                e.wait_ge(veV.h, b["veV"] + 2)
                e.dma_start(out=t["CG0"][:, :], in_=t["cstricttg"].ap()
                            ).then_inc(sqL.h, 16)
                e.wait_ge(veV.h, b["veV"] + 3)
                e.dma_start(out=t["TQAp"][:, 0:64], in_=t["ceye128"].ap()
                            ).then_inc(sqL.h, 16)
                e.wait_ge(veV.h, b["veV"] + 4)
                e.dma_start(out=t["TQAp"][:, 128:256], in_=t["ceyef"].ap()
                            ).then_inc(sqL.h, 16)
                sqL.n = b["sqL"] + 16 * 7

            @blk.scalar
            def _(e):
                e.wait_ge(sqL.h, b["sqL"] + 16 * 1)
                e.wait_ge(self.gqB.h, b["gqB"])
                for h in range(H):
                    e.activation(t["CG1"][:, h * 256:(h + 1) * 256],
                                 t["CG0"][:, h * 256:(h + 1) * 256], AF.Exp,
                                 scale=t["LGB"][:, h:h + 1]).then_inc(scA.h, 1)
                e.wait_ge(sqL.h, b["sqL"] + 16 * 3)
                e.wait_ge(veV.h, b["veV"] + 1)
                for h in range(H):
                    e.activation(t["CG1"][:, h * 256:(h + 1) * 256],
                                 t["CG0"][:, h * 256:(h + 1) * 256], AF.Exp,
                                 scale=t["LGB"][:, h:h + 1]).then_inc(scA.h, 1)
                scA.n = b["scA"] + 8

            @blk.vector
            def _(e):
                e.wait_ge(sqL.h, b["sqL"] + 16 * 2)
                e.wait_ge(scA.h, b["scA"] + 4)
                e.tensor_tensor(t["NEGLS"][:, :], t["CG1"][:, :], t["CG0"][:, :],
                                OP.mult).then_inc(veV.h, 1)
                e.wait_ge(sqL.h, b["sqL"] + 16 * 4)
                e.wait_ge(scA.h, b["scA"] + 8)
                e.tensor_tensor(t["NEGLST"][:, :], t["CG1"][:, :], t["CG0"][:, :],
                                OP.mult).then_inc(veV.h, 1)
                e.wait_ge(sqL.h, b["sqL"] + 16 * 5)
                e.tensor_tensor(t["ALPHALMT"][:, :], t["CG1"][:, :],
                                t["CG0"][:, :], OP.mult)
                for h in range(H):
                    li = e.tensor_scalar(t["ALPHALMT"][:, h * 256:(h + 1) * 256],
                                         t["ALPHALMT"][:, h * 256:(h + 1) * 256],
                                         t["ALB"][:, h:h + 1], -1.0,
                                         OP.mult, OP.mult)
                li.then_inc(veV.h, 1)
                e.wait_ge(sqL.h, b["sqL"] + 16 * 6)
                e.tensor_copy(t["EYE128"][:, :], t["TQAp"][:, 0:64]
                              ).then_inc(veV.h, 1)
                e.wait_ge(sqL.h, b["sqL"] + 16 * 7)
                e.tensor_copy(t["EYEF"][:, :], t["TQAp"][:, 128:256])
                e.tensor_scalar_mul(t["GT128"][:, :], t["GT128"][:, :], -1.0)
                veV.n = b["veV"] + 4

    def p0_weights(self):
        self._quant_w2()
        self._quant_w1()
        self._weight_cast()

    def _quant_w2(self):
        nc, t = self.nc, self.t
        sqL, sqT, gqS, veV, scA = self.sqL, self.sqT, self.gqS, self.veV, self.scA
        mmG = self.mmG
        PB = self.PB
        b = self.bases()
        with nc.Block() as blk:
            @blk.sync
            def _(e):
                for j in range(0, ND, 2):
                    if j >= 2:
                        e.wait_ge(scA.h, b["scA"] + (j - 1))
                    e.dma_start(out=t["WT0"][:, :],
                                in_=t["w2"][j * 128:(j + 1) * 128, :]
                                ).then_inc(self.dsl[0].h, 16)
                    self.dsl[0].n += 16

            @blk.gpsimd
            def _(e):
                for j in range(1, ND, 2):
                    if j >= 2:
                        e.wait_ge(scA.h, b["scA"] + (j - 1))
                    e.dma_start(out=t["WT1"][:, :],
                                in_=t["w2"][j * 128:(j + 1) * 128, :]
                                ).then_inc(self.dsl[1].h, 16)
                    self.dsl[1].n += 16

            @blk.vector
            def _(e):
                def recip(j):
                    buf = t["WT0"] if j % 2 == 0 else t["WT1"]
                    e.wait_ge(self.dsl[j % 2].h,
                              b["dsl%d" % (j % 2)] + 16 * (j // 2 + 1))
                    self.fv(e, e.reduce_max(t["SWV2"][:, j:j + 1], buf[:, :],
                                            axis=AX, apply_absolute_value=True))
                    self.fv(e, e.tensor_scalar_mul(t["SWV2"][:, j:j + 1],
                                                   t["SWV2"][:, j:j + 1],
                                                   1.0 / 127.0))
                    e.reciprocal(t["QT5"][:, j:j + 1], t["SWV2"][:, j:j + 1]
                                 ).then_inc(veV.h, 1)

                recip(0)
                recip(1)
                for j in range(ND):
                    tq = t["TQAp"] if j % 2 == 0 else t["TQBp"]
                    e.wait_ge(scA.h, b["scA"] + (j + 1))
                    e.tensor_scalar_add(self.wbuf[j][:, :], tq[:, :],
                                        -MAGIC).then_inc(veV.h, 1)
                    if j + 2 < ND:
                        recip(j + 2)
                veV.n = b["veV"] + 2 * ND

            @blk.scalar
            def _(e):
                for j in range(ND):
                    buf = t["WT0"] if j % 2 == 0 else t["WT1"]
                    tq = t["TQAp"] if j % 2 == 0 else t["TQBp"]
                    e.wait_ge(veV.h, b["veV"] + (j + 1 if j < 2 else 2 * j))
                    e.activation(tq[:, :], buf[:, :], AF.Identity,
                                 scale=t["QT5"][:, j:j + 1], bias=MAGIC
                                 ).then_inc(scA.h, 1)
                scA.n = b["scA"] + ND

        b = self.bases()
        with nc.Block() as blk:
            @blk.tensor
            def _(e):
                for j2 in range(NF):
                    for w in range(2):
                        wid = j2 * 2 + w
                        if j2 >= 2:
                            e.wait_ge(scA.h, b["scA"] + 2 * (j2 - 2) + w + 1)
                        bk = (j2 % 2) * 2 + w
                        for cq in range(4):
                            c = w * 4 + cq
                            mi = e.matmul(
                                PB[bk][:, cq * 128:(cq + 1) * 128],
                                self.wbuf[c][:, j2 * 128:(j2 + 1) * 128],
                                t["EYEF"][:, :], start=True, stop=True)
                        mi.then_inc(mmG.h, 1)
                mmG.n = b["mmG"] + 2 * NF

            @blk.scalar
            def _(e):
                for j2 in range(NF):
                    stg = t["W2T1"] if j2 % 2 == 0 else t["W2T2"]
                    for w in range(2):
                        wid = j2 * 2 + w
                        e.wait_ge(mmG.h, b["mmG"] + wid + 1)
                        if w == 0 and j2 >= 2:
                            e.wait_ge(self.dstp[j2 % 2].h,
                                      b["dstp%d" % (j2 % 2)]
                                      + 16 * ((j2 - 2) // 2 + 1))
                        bk = (j2 % 2) * 2 + w
                        e.activation(stg[:, w * 512:(w + 1) * 512],
                                     PB[bk][:, 0:512], AF.Identity
                                     ).then_inc(scA.h, 1)
                scA.n = b["scA"] + 2 * NF

            @blk.gpsimd
            def _(e):
                for j2 in range(NF):
                    stg = t["W2T1"] if j2 % 2 == 0 else t["W2T2"]
                    e.wait_ge(scA.h, b["scA"] + 2 * (j2 + 1))
                    e.dma_start(out=t["w2q"][j2 * 128:(j2 + 1) * 128, :],
                                in_=stg[:, :]).then_inc(self.dstp[j2 % 2].h, 16)
                    self.dstp[j2 % 2].n += 16

    def _quant_w1(self):
        nc, t = self.nc, self.t
        gqS, veV, scA, mmG = self.gqS, self.veV, self.scA, self.mmG
        PB = self.PB
        b = self.bases()
        with nc.Block() as blk:
            @blk.sync
            def _(e):
                for j in range(NF):
                    if j % 4 >= 2:
                        continue
                    dst = t["WT0"][:, (j % 4) * 1024:(j % 4 + 1) * 1024]
                    if j >= 4:
                        e.wait_ge(scA.h, b["scA"] + 3 * j - 11)
                    e.dma_start(out=dst, in_=t["w1"][j * 128:(j + 1) * 128, :]
                                ).then_inc(self.dsl[j % 4].h, 16)
                    self.dsl[j % 4].n += 16

            @blk.vector
            def _(e):
                def recip(j):
                    src_ = t["WT0"][:, (j % 4) * 1024:(j % 4 + 1) * 1024]
                    e.wait_ge(self.dsl[j % 4].h,
                              b["dsl%d" % (j % 4)] + 16 * (j // 4 + 1))
                    self.fv(e, e.reduce_max(t["SWV1"][:, j:j + 1], src_,
                                            axis=AX, apply_absolute_value=True))
                    self.fv(e, e.tensor_scalar_mul(t["SWV1"][:, j:j + 1],
                                                   t["SWV1"][:, j:j + 1],
                                                   1.0 / 127.0))
                    e.reciprocal(t["QT5"][:, j:j + 1], t["SWV1"][:, j:j + 1]
                                 ).then_inc(veV.h, 1)

                recip(0)
                recip(1)
                for j in range(NF):
                    e.wait_ge(scA.h, b["scA"] + 3 * j + 1)
                    stg = t["W2T1"] if j % 2 == 0 else t["W2T2"]
                    tq = t["TQAp"][:, (j % 2) * 1024:(j % 2 + 1) * 1024]
                    if j >= 2:
                        e.wait_ge(mmG.h, b["mmG"] + 2 * j - 2)
                    e.tensor_scalar_add(stg[:, :], tq,
                                        -MAGIC).then_inc(veV.h, 1)
                    if j + 2 < NF:
                        recip(j + 2)
                veV.n = b["veV"] + 2 * NF

            @blk.scalar
            def _(e):
                for j in range(NF):
                    src_ = t["WT0"][:, (j % 4) * 1024:(j % 4 + 1) * 1024]
                    tq = t["TQAp"][:, (j % 2) * 1024:(j % 2 + 1) * 1024]
                    e.wait_ge(veV.h, b["veV"] + (j + 1 if j < 2 else 2 * j))
                    e.activation(tq, src_, AF.Identity,
                                 scale=t["QT5"][:, j:j + 1], bias=MAGIC
                                 ).then_inc(scA.h, 1)
                    for w in range(2):
                        e.wait_ge(mmG.h, b["mmG"] + 2 * j + w + 1)
                        for cq in range(4):
                            c = w * 4 + cq
                            li = e.activation(
                                self.wbuf[c][:, j * 128:(j + 1) * 128],
                                PB[(j % 2) * 4 + cq][:, 0:128], AF.Identity)
                        li.then_inc(scA.h, 1)
                scA.n = b["scA"] + 3 * NF

            @blk.tensor
            def _(e):
                def tt(j):
                    return 2 * j + 3 if j <= NF - 3 else 2 * NF - (NF - 1 - j)

                for j in range(NF):
                    stg = t["W2T1"] if j % 2 == 0 else t["W2T2"]
                    e.wait_ge(veV.h, b["veV"] + tt(j))
                    for w in range(2):
                        if w == 1:
                            e.wait_ge(scA.h, b["scA"] + 3 * j + 2)
                        elif j >= 2:
                            e.wait_ge(scA.h, b["scA"] + 3 * j - 3)
                        for cq in range(4):
                            c = w * 4 + cq
                            mi = e.matmul(PB[(j % 2) * 4 + cq][:, 0:128],
                                          stg[:, c * 128:(c + 1) * 128],
                                          t["EYEF"][:, :], start=True, stop=True)
                        mi.then_inc(mmG.h, 1)
                mmG.n = b["mmG"] + 2 * NF

            @blk.gpsimd
            def _(e):
                for j in range(NF):
                    if j % 4 < 2:
                        continue
                    dst = t["WT0"][:, (j % 4) * 1024:(j % 4 + 1) * 1024]
                    if j >= 4:
                        e.wait_ge(scA.h, b["scA"] + 3 * j - 11)
                    e.dma_start(out=dst, in_=t["w1"][j * 128:(j + 1) * 128, :]
                                ).then_inc(self.dsl[j % 4].h, 16)
                    self.dsl[j % 4].n += 16
                e.wait_ge(scA.h, b["scA"] + 3 * NF)
                for c in range(ND):
                    e.dma_start(out=t["w1q"][c * 128:(c + 1) * 128, :],
                                in_=self.wbuf[c][:, :]).then_inc(gqS.h, 16)
                gqS.n = b["gqS"] + 16 * ND

    def _weight_cast(self):
        nc, t = self.nc, self.t
        veV = self.veV
        b = self.bases()
        items = [("wwt", "wwtb", c) for c in range(ND)] + \
                [("wrt", "wrtb", c) for c in range(ND)]
        with nc.Block() as blk:
            @blk.sync
            def _(e):
                for i, (src, dst, c) in enumerate(items):
                    if i >= 4:
                        e.wait_ge(veV.h, b["veV"] + (i - 3))
                    e.dma_start(out=t["WT0"][:, (i % 4) * 1024:
                                             (i % 4 + 1) * 1024],
                                in_=t[src][c * 128:(c + 1) * 128, :]
                                ).then_inc(self.dsl[i % 4].h, 16)
                    self.dsl[i % 4].n += 16

            @blk.vector
            def _(e):
                for i, (src, dst, c) in enumerate(items):
                    e.wait_ge(self.dsl[i % 4].h,
                              b["dsl%d" % (i % 4)] + 16 * (i // 4 + 1))
                    if i >= 4:
                        e.wait_ge(self.hts[i % 4].h,
                                  b["hts%d" % (i % 4)]
                                  + 16 * ((i - 4) // 4 + 1))
                    e.tensor_copy(t["WCST"][:, (i % 4) * 1024:
                                            (i % 4 + 1) * 1024],
                                  t["WT0"][:, (i % 4) * 1024:
                                           (i % 4 + 1) * 1024]
                                  ).then_inc(veV.h, 1)
                veV.n = b["veV"] + len(items)

            @blk.gpsimd
            def _(e):
                for i, (src, dst, c) in enumerate(items):
                    e.wait_ge(veV.h, b["veV"] + (i + 1))
                    e.dma_start(out=t[dst][c * 128:(c + 1) * 128, :],
                                in_=t["WCST"][:, (i % 4) * 1024:
                                              (i % 4 + 1) * 1024]
                                ).then_inc(self.hts[i % 4].h, 16)
                    self.hts[i % 4].n += 16

    def stats_ln(self, src, with_max, per_head, ccidx=None):
        nc, t = self.nc, self.t
        gqS, gqB, ccS, mmG, veV, scA = (self.gqS, self.gqB, self.ccS, self.mmG,
                                        self.veV, self.scA)
        snG = self.snG
        PB = self.PB
        b = self.bases()
        SRC = t[src]
        dln, drn = self.dln, self.drn
        nA = 3 + (3 if per_head else 0)
        nV = 3 + (1 if with_max else 0) + (2 if per_head else 0)
        nM = 1 + (1 if per_head else 0)
        with nc.Block() as blk:
            @blk.sync
            def _(e):
                self.entry_waits(e, b)
                for g in range(NTG):
                    if g >= 2:
                        e.wait_ge(veV.h, b["veV"] + nV * (g - 1))
                        e.wait_ge(scA.h, b["scA"] + nA * (g - 2) + 2)
                    xa = t["XA0"] if g % 2 == 0 else t["XA1"]
                    e.dma_start(out=xa[:, :],
                                in_=bass.AP(SRC, g * TG,
                                            [[R, 128], [128 * R, ND], [1, TG]])
                                ).then_inc(self.dsl[g % 2].h, 16)
                    self.dsl[g % 2].n += 16

            @blk.scalar
            def _(e):
                for g in range(NTG):
                    xa = t["XA0"] if g % 2 == 0 else t["XA1"]
                    tb = t["TB0"] if g % 2 == 0 else t["TB1"]
                    vb = t["VB0"] if g % 2 == 0 else t["VB1"]
                    e.wait_ge(self.dsl[g % 2].h,
                              b["dsl%d" % (g % 2)] + 16 * (g // 2 + 1))
                    if g >= 1:
                        if per_head:
                            e.wait_ge(drn.h, b["drn"] + 32 * g)
                        else:
                            e.wait_ge(mmG.h, b["mmG"] + nM * g)
                    e.activation(t["XBF"][:, :], xa[:, :], AF.Identity
                                 ).then_inc(scA.h, 1)
                    e.activation(t["XSQ"][:, :], xa[:, :], AF.Square
                                 ).then_inc(scA.h, 1)
                    e.wait_ge(veV.h, b["veV"] + nV * g + 1)
                    e.activation(tb[:, :], vb[:, :], AF.Sqrt).then_inc(scA.h, 1)
                    if per_head:
                        e.wait_ge(veV.h, b["veV"] + nV * g + 3)
                        e.wait_ge(mmG.h, b["mmG"] + nM * g + 1)
                        e.activation(t["XSQ"][:, :], t["LNT0"][:, :], AF.Square
                                     ).then_inc(scA.h, 1)
                        e.wait_ge(mmG.h, b["mmG"] + nM * g + 2)
                        if g >= 1:
                            e.wait_ge(veV.h, b["veV"] + nV * (g - 1) + 5)
                        for hh in range(H):
                            li = e.activation(t["RN%d" % hh][:, :],
                                              PB[2 + hh][:, :], AF.Sqrt)
                        li.then_inc(scA.h, 1)
                        e.activation(t["UBT"][:, :], t["LNT0"][:, :],
                                     AF.Identity).then_inc(scA.h, 1)
                scA.n = b["scA"] + nA * NTG

            @blk.vector
            def _(e):
                for g in range(NTG):
                    xa = t["XA0"] if g % 2 == 0 else t["XA1"]
                    mb = t["MB0"] if g % 2 == 0 else t["MB1"]
                    vb = t["VB0"] if g % 2 == 0 else t["VB1"]
                    tb = t["TB0"] if g % 2 == 0 else t["TB1"]
                    rb = t["RB0"] if g % 2 == 0 else t["RB1"]
                    mrb = t["MRB0"] if g % 2 == 0 else t["MRB1"]
                    e.wait_ge(mmG.h, b["mmG"] + nM * g + 1)
                    e.tensor_scalar_mul(mb[:, :], PB[0][:, :], 1.0 / D)
                    e.tensor_tensor(tb[:, :], mb[:, :], mb[:, :], OP.mult)
                    e.tensor_scalar_mul(vb[:, :], PB[1][:, :], 1.0 / D)
                    e.tensor_sub(vb[:, :], vb[:, :], tb[:, :])
                    e.tensor_scalar_add(vb[:, :], vb[:, :], EPSLN
                                        ).then_inc(veV.h, 1)
                    e.wait_ge(scA.h, b["scA"] + nA * g + 3)
                    self.fv(e, e.reciprocal(rb[:, :], tb[:, :]))
                    e.tensor_tensor(mrb[:, :], mb[:, :], rb[:, :], OP.mult
                                    ).then_inc(veV.h, 1)
                    if g >= 1:
                        e.wait_ge(dln.h, b["dln"] + 16 * g)
                        if per_head:
                            e.wait_ge(scA.h, b["scA"] + nA * (g - 1) + 6)
                    for c in range(ND):
                        e.tensor_tensor(t["LNT0"][:, c * TG:(c + 1) * TG],
                                        xa[:, c * TG:(c + 1) * TG], rb[:, :],
                                        OP.mult)
                    for c in range(ND):
                        ln_i = e.tensor_tensor(t["LNT0"][:, c * TG:(c + 1) * TG],
                                               t["LNT0"][:, c * TG:(c + 1) * TG],
                                               mrb[:, :], OP.subtract)
                    ln_i.then_inc(veV.h, 1)
                    if with_max:
                        self.fv(e, e.reduce_max(t["TMPC"][:, 0:1],
                                                t["LNT0"][:, :], axis=AX))
                        e.tensor_max(t["RMAX"][:, 0:1], t["RMAX"][:, 0:1],
                                     t["TMPC"][:, 0:1]).then_inc(veV.h, 1)
                    if per_head:
                        e.wait_ge(scA.h, b["scA"] + nA * g + 5)
                        self.fv(e, e.tensor_scalar_max(t["RNALL"][:, :],
                                                       t["RNALL"][:, :],
                                                       1e-12))
                        e.reciprocal(t["RNALL"][:, :], t["RNALL"][:, :]
                                     ).then_inc(veV.h, 1)
                        if g >= 1:
                            e.wait_ge(drn.h, b["drn"] + 32 * g - 16)
                        for c in range(ND):
                            li = e.tensor_tensor(
                                t["XSQ"][:, c * TG:(c + 1) * TG],
                                t["LNT0"][:, c * TG:(c + 1) * TG],
                                t["RN%d" % (c // 2)][:, :], OP.mult)
                        li.then_inc(veV.h, 1)
                if ccidx is not None:
                    e.wait_ge(gqB.h, b["gqB"] + 16)
                    self.fv(e, e.memset(t["CCS"][0:1, :], NEGBIG))
                    e.reduce_max(t["CCS"][0:1, 0:1], t["MROWS"][0:1, :],
                                 axis=AX).then_inc(veV.h, 1)
                    e.wait_ge(gqB.h, b["gqB"] + 32)
                    self.fv(e, e.tensor_scalar_max(t["S1C"][:, 0:1],
                                                   t["SCC128"][:, 0:1], EPSQ))
                    self.fv(e, e.tensor_scalar_mul(t["S1C"][:, 0:1],
                                                   t["S1C"][:, 0:1],
                                                   1.0 / 255.0))
                    e.reciprocal(t["IS1C"][:, 0:1], t["S1C"][:, 0:1]
                                 ).then_inc(veV.h, 1)
                    e.memset(t["RMAX"][:, :], NEGBIG)
                    veV.n = b["veV"] + nV * NTG + 2
                else:
                    veV.n = b["veV"] + nV * NTG

            @blk.tensor
            def _(e):
                for g in range(NTG):
                    e.wait_ge(scA.h, b["scA"] + nA * g + 2)
                    if g >= 1:
                        e.wait_ge(veV.h, b["veV"] + nV * (g - 1) + 1)
                    for c in range(ND):
                        e.matmul(PB[0][:, :], t["ONES128"][:, :],
                                 t["XBF"][:, c * TG:(c + 1) * TG],
                                 start=(c == 0), stop=(c == ND - 1))
                    for c in range(ND):
                        mi = e.matmul(PB[1][:, :], t["ONES128"][:, :],
                                      t["XSQ"][:, c * TG:(c + 1) * TG],
                                      start=(c == 0), stop=(c == ND - 1))
                    mi.then_inc(mmG.h, 1)
                    if per_head:
                        e.wait_ge(scA.h, b["scA"] + nA * g + 4)
                        if g >= 1:
                            e.wait_ge(veV.h, b["veV"] + nV * (g - 1) + 5)
                        for hh in range(H):
                            for j in range(2):
                                c = 2 * hh + j
                                mi = e.matmul(PB[2 + hh][:, :],
                                              t["ONES128"][:, :],
                                              t["XSQ"][:, c * TG:(c + 1) * TG],
                                              start=(j == 0), stop=(j == 1))
                        mi.then_inc(mmG.h, 1)
                mmG.n = b["mmG"] + nM * NTG

            @blk.gpsimd
            def _(e):
                for g in range(NTG):
                    e.wait_ge(veV.h, b["veV"] + nV * g + 3)
                    e.dma_start(out=bass.AP(t["lnbuf"], g * TG,
                                            [[R, 128], [128 * R, ND], [1, TG]]),
                                in_=t["LNT0"][:, :]).then_inc(dln.h, 16)
                    dln.n += 16
                    if ccidx is not None and g == NTG - 1:
                        cci = t["cci%d" % ccidx]
                        cco = t["cco%d" % ccidx]
                        e.wait_ge(veV.h, b["veV"] + nV * NTG)
                        e.dma_start(out=t["MROWS"][0:1, :],
                                    in_=t["RMAX"][:, 0:1]).then_inc(gqB.h, 16)
                        gqB.n += 16
                        e.wait_ge(veV.h, b["veV"] + nV * NTG + 1)
                        e.dma_start(out=cci.ap(), in_=t["CCS"][0:1, :]
                                    ).then_inc(gqS.h, 16)
                        gqS.n += 16
                        e.wait_ge(gqS.h, gqS.n)
                        e.collective_compute(
                            "AllReduce", OP.max,
                            replica_groups=[list(range(8))],
                            ins=[cci.ap().opt()],
                            outs=[cco.ap().opt()],
                        ).then_inc(ccS.h, 1)
                        ccS.n += 1
                        e.wait_ge(ccS.h, ccS.n)
                        e.dma_start(out=t["SCC128"][:, 0:1],
                                    in_=bass.AP(cco, 0, [[0, 128], [1, 1]])
                                    ).then_inc(gqB.h, 16)
                        gqB.n += 16
                    if per_head:
                        e.wait_ge(veV.h, b["veV"] + nV * g + 5)
                        e.dma_start(out=bass.AP(t["rkbuf"], g * TG,
                                                [[R, 128], [128 * R, ND],
                                                 [1, TG]]),
                                    in_=t["XSQ"][:, :]).then_inc(drn.h, 16)
                        e.wait_ge(scA.h, b["scA"] + nA * g + 6)
                        e.dma_start(out=bass.AP(t["ubbuf"], g * TG,
                                                [[R, 128], [128 * R, ND],
                                                 [1, TG]]),
                                    in_=t["UBT"][:, :]).then_inc(drn.h, 16)
                        drn.n += 32

    def qscale(self, ccidx, maxsrc, swname, ntile, ssw, bsrc, bq, with_half):
        nc, t = self.nc, self.t
        gqS, gqB, ccS, veV, scA = self.gqS, self.gqB, self.ccS, self.veV, self.scA
        b = self.bases()
        cci, cco = t["cci%d" % ccidx], t["cco%d" % ccidx]
        with nc.Block() as blk:
            @blk.gpsimd
            def _(e):
                e.dma_start(out=bass.AP(t["mrow"], 0, [[1, 128], [1, 1]]),
                            in_=t[maxsrc][:, 0:1]).then_inc(gqS.h, 16)
                e.wait_ge(gqS.h, b["gqS"] + 16)
                e.dma_start(out=t["MROWS"][0:1, :], in_=t["mrow"].ap()
                            ).then_inc(gqB.h, 16)
                e.wait_ge(veV.h, b["veV"] + 1)
                e.dma_start(out=cci.ap(), in_=t["CCS"][0:1, :]).then_inc(gqS.h, 16)
                e.wait_ge(gqS.h, b["gqS"] + 32)
                e.collective_compute(
                    "AllReduce", OP.max,
                    replica_groups=[list(range(8))],
                    ins=[cci.ap().opt()],
                    outs=[cco.ap().opt()],
                ).then_inc(ccS.h, 1)
                e.wait_ge(ccS.h, b["ccS"] + 1)
                e.dma_start(out=t["SCC"][0:1, :], in_=cco.ap()).then_inc(gqB.h, 16)
                gqS.n = b["gqS"] + 32
                gqB.n = b["gqB"] + 32
                ccS.n = b["ccS"] + 1
                e.wait_ge(veV.h, b["veV"] + 2)
                e.dma_start(out=bass.AP(t["misc2"], 0, [[2, 1], [1, 2]]),
                            in_=t["TB0"][0:1, 0:2]).then_inc(gqS.h, 16)
                gqS.n += 16
                e.wait_ge(gqS.h, gqS.n)
                e.dma_start(out=t["S1C"][:, 0:1],
                            in_=bass.AP(t["misc2"], 0, [[0, 128], [1, 1]])
                            ).then_inc(gqB.h, 16)
                e.dma_start(out=t["IS1C"][:, 0:1],
                            in_=bass.AP(t["misc2"], 1, [[0, 128], [1, 1]])
                            ).then_inc(gqB.h, 16)
                gqB.n += 32

            @blk.vector
            def _(e):
                e.wait_ge(gqB.h, b["gqB"] + 16)
                self.fv(e, e.memset(t["CCS"][0:1, :], NEGBIG))
                e.reduce_max(t["CCS"][0:1, 0:1], t["MROWS"][0:1, :], axis=AX
                             ).then_inc(veV.h, 1)
                e.wait_ge(gqB.h, b["gqB"] + 32)
                self.fv(e, e.tensor_scalar_max(t["TB0"][0:1, 0:1],
                                               t["SCC"][0:1, 0:1], EPSQ))
                self.fv(e, e.tensor_scalar_mul(t["TB0"][0:1, 0:1],
                                               t["TB0"][0:1, 0:1], 1.0 / 255.0))
                e.reciprocal(t["TB0"][0:1, 1:2], t["TB0"][0:1, 0:1]
                             ).then_inc(veV.h, 1)
                e.wait_ge(gqB.h, b["gqB"] + 64)
                self.fv(e, e.tensor_scalar_mul(t[ssw][:, 0:ntile],
                                               t[swname][:, 0:ntile],
                                               t["S1C"][:, 0:1]))
                self.fv(e, e.reciprocal(t["QT5"][:, 0:ntile], t[ssw][:, 0:ntile]))
                self.fv(e, e.tensor_tensor(t["QT6"][:, 0:ntile],
                                           t[bsrc][:, 0:ntile],
                                           t["QT5"][:, 0:ntile], OP.mult))
                self.fv(e, e.tensor_scalar_add(t["QT7"][:, 0:ntile],
                                               t["QT6"][:, 0:ntile], MAGIC))
                e.tensor_scalar_add(t["QT7"][:, 0:ntile], t["QT7"][:, 0:ntile],
                                    -MAGIC).then_inc(veV.h, 1)
                e.wait_ge(scA.h, b["scA"] + 1)
                self.fv(e, e.tensor_single_scalar(t["QTM"][:, 0:ntile],
                                                  t["QT8"][:, 0:ntile],
                                                  RRLIM, OP.is_lt))
                self.fv(e, e.select(t["QT6"][:, 0:ntile], t["QTM"][:, 0:ntile],
                                    t["QT7"][:, 0:ntile], t["QT6"][:, 0:ntile]))
                self.fv(e, e.tensor_tensor(t[bq][:, 0:ntile], t["QT6"][:, 0:ntile],
                                           t[ssw][:, 0:ntile], OP.mult))
                if with_half:
                    self.fv(e, e.tensor_scalar_mul(t[ssw][:, 0:ntile],
                                                   t[ssw][:, 0:ntile], 0.5))
                    li = e.tensor_scalar_mul(t[bq][:, 0:ntile], t[bq][:, 0:ntile],
                                             0.5)
                else:
                    li = e.tensor_scalar_mul(t[bq][:, 0:ntile], t[bq][:, 0:ntile],
                                             1.0)
                li.then_inc(veV.h, 1)
                veV.n = b["veV"] + 4

            @blk.scalar
            def _(e):
                e.wait_ge(veV.h, b["veV"] + 3)
                e.activation(t["QT8"][:, 0:ntile], t["QT6"][:, 0:ntile], AF.Abs
                             ).then_inc(scA.h, 1)
                scA.n = b["scA"] + 1

    def mat1(self, ccidx=None):
        nc, t = self.nc, self.t
        sqL, gqS, gqB, ccS, mmG, veV, scA = (self.sqL, self.gqS, self.gqB,
                                             self.ccS, self.mmG, self.veV,
                                             self.scA)
        PB = self.PB
        b = self.bases()
        vO = 1

        def act2_t(g):
            return 2 * (g + 1) + 32 * max(g - 1, 0)

        def evac_t(gi):
            g, ft = divmod(gi, NF)
            if g < NTG - 1:
                return 2 * (g + 2) + 32 * g + ft + 1
            return 2 * NTG + 32 * (NTG - 1) + ft + 1

        def emit_evacs(e, g):
            for ft in range(NF):
                gi = g * NF + ft
                e.wait_ge(mmG.h, b["mmG"] + gi + 1)
                if gi == 0:
                    e.wait_ge(veV.h, b["veV"] + vO)
                if gi >= 4:
                    e.wait_ge(self.hts[ft % 4].h,
                              b["hts%d" % (ft % 4)] + 16 * (gi // 4))
                    e.wait_ge(veV.h, b["veV"] + vO + (gi - 3))
                e.activation(t["HT%d" % (ft % 4)][:, :], PB[ft % 4][:, :],
                             AF.Relu, scale=t["S1SW"][:, ft:ft + 1],
                             bias=t["B1Q"][:, ft:ft + 1]).then_inc(scA.h, 1)

        with nc.Block() as blk:
            @blk.sync
            def _(e):
                self.entry_waits(e, b)
                for g in range(NTG):
                    if g >= 2:
                        e.wait_ge(scA.h, b["scA"] + act2_t(g - 2))
                    xa = t["MA0"] if g % 2 == 0 else t["MA1"]
                    e.dma_start(out=xa[:, :],
                                in_=bass.AP(t["lnbuf"], g * TG,
                                            [[R, 128], [128 * R, ND], [1, TG]])
                                ).then_inc(self.dsl[g % 2].h, 16)
                    self.dsl[g % 2].n += 16

            @blk.scalar
            def _(e):
                for g in range(NTG):
                    xa = t["MA0"] if g % 2 == 0 else t["MA1"]
                    qx = t["QX0"] if g % 2 == 0 else t["QX1"]
                    e.wait_ge(self.dsl[g % 2].h,
                              b["dsl%d" % (g % 2)] + 16 * (g // 2 + 1))
                    if g >= 2:
                        e.wait_ge(mmG.h, b["mmG"] + 32 * (g - 1))
                    e.activation(t["MTQ"][:, :], xa[:, :], AF.Identity,
                                 scale=t["IS1C"][:, 0:1], bias=MAGIC
                                 ).then_inc(scA.h, 1)
                    e.activation(qx[:, :], t["MTQ"][:, :], AF.Relu, bias=-MAGIC
                                 ).then_inc(scA.h, 1)
                    if g >= 1:
                        emit_evacs(e, g - 1)
                emit_evacs(e, NTG - 1)
                scA.n = b["scA"] + 34 * NTG

            @blk.tensor
            def _(e):
                e.wait_ge(self.sqL.h, b["sqL"])
                for g in range(NTG):
                    qx = t["QX0"] if g % 2 == 0 else t["QX1"]
                    e.wait_ge(scA.h, b["scA"] + act2_t(g))
                    for ft in range(NF):
                        gi = g * NF + ft
                        if gi >= 4:
                            e.wait_ge(scA.h, b["scA"] + evac_t(gi - 4))
                        for k in range(ND):
                            mi = e.matmul(PB[ft % 4][:, :],
                                          self.wbuf[k][:, ft * 128:(ft + 1) * 128],
                                          qx[:, k * TG:(k + 1) * TG],
                                          start=(k == 0), stop=(k == ND - 1))
                        mi.then_inc(mmG.h, 1)
                mmG.n = b["mmG"] + 32 * NTG

            @blk.vector
            def _(e):
                self.derive_q(e, "SWV1", 32, "S1SW", "B1S", "B1Q", False)
                e.tensor_copy(t["TMPC"][:, 0:1], t["TMPC"][:, 0:1]
                              ).then_inc(veV.h, 1)
                for g in range(NTG):
                    for ft in range(NF):
                        gi = g * NF + ft
                        e.wait_ge(scA.h, b["scA"] + evac_t(gi))
                        self.fv(e, e.reduce_max(t["TMPC"][:, 0:1],
                                                t["HT%d" % (ft % 4)][:, :],
                                                axis=AX))
                        e.tensor_max(t["HMAX"][:, 0:1], t["HMAX"][:, 0:1],
                                     t["TMPC"][:, 0:1]).then_inc(veV.h, 1)
                if ccidx is not None:
                    e.wait_ge(gqB.h, b["gqB"] + 16)
                    self.fv(e, e.memset(t["CCS"][0:1, :], NEGBIG))
                    e.reduce_max(t["CCS"][0:1, 0:1], t["MROWS"][0:1, :],
                                 axis=AX).then_inc(veV.h, 1)
                    e.wait_ge(gqB.h, b["gqB"] + 32)
                    self.fv(e, e.tensor_scalar_max(t["S1C"][:, 0:1],
                                                   t["SCC128"][:, 0:1], EPSQ))
                    self.fv(e, e.tensor_scalar_mul(t["S1C"][:, 0:1],
                                                   t["S1C"][:, 0:1],
                                                   1.0 / 255.0))
                    e.reciprocal(t["IS1C"][:, 0:1], t["S1C"][:, 0:1]
                                 ).then_inc(veV.h, 1)
                    e.memset(t["HMAX"][:, :], NEGBIG)
                    veV.n = b["veV"] + vO + 32 * NTG + 2
                else:
                    veV.n = b["veV"] + vO + 32 * NTG

            @blk.gpsimd
            def _(e):
                for g in range(NTG):
                    for ft in range(NF):
                        gi = g * NF + ft
                        e.wait_ge(scA.h, b["scA"] + evac_t(gi))
                        e.dma_start(out=t["hbuf"][ft * 128:(ft + 1) * 128,
                                                  g * TG:(g + 1) * TG],
                                    in_=t["HT%d" % (ft % 4)][:, :]
                                    ).then_inc(self.hts[ft % 4].h, 16)
                        self.hts[ft % 4].n += 16
                if ccidx is not None:
                    cci = t["cci%d" % ccidx]
                    cco = t["cco%d" % ccidx]
                    e.wait_ge(veV.h, b["veV"] + vO + 32 * NTG)
                    e.dma_start(out=t["MROWS"][0:1, :],
                                in_=t["HMAX"][:, 0:1]).then_inc(gqB.h, 16)
                    gqB.n += 16
                    e.wait_ge(veV.h, b["veV"] + vO + 32 * NTG + 1)
                    e.dma_start(out=cci.ap(), in_=t["CCS"][0:1, :]
                                ).then_inc(gqS.h, 16)
                    gqS.n += 16
                    e.wait_ge(gqS.h, gqS.n)
                    e.collective_compute(
                        "AllReduce", OP.max,
                        replica_groups=[list(range(8))],
                        ins=[cci.ap().opt()],
                        outs=[cco.ap().opt()],
                    ).then_inc(ccS.h, 1)
                    ccS.n += 1
                    e.wait_ge(ccS.h, ccS.n)
                    e.dma_start(out=t["SCC128"][:, 0:1],
                                in_=bass.AP(cco, 0, [[0, 128], [1, 1]])
                                ).then_inc(gqB.h, 16)
                    gqB.n += 16

    def mat2(self, xsrc, dst):
        nc, t = self.nc, self.t
        sqL, gqS, gqB, mmG, veV, scA = (self.sqL, self.gqS, self.gqB, self.mmG,
                                        self.veV, self.scA)
        PB = self.PB
        b = self.bases()
        NG2 = 16
        T2 = 256
        vO = 1

        def act1_t(g, q):
            return (q + 1) if g == 0 else 16 * g - 8 + q + 1

        def evac_t(g, dt):
            return (16 * g + 16 + dt + 1) if g < NG2 - 1 \
                else 16 * NG2 - 8 + dt + 1

        with nc.Block() as blk:
            @blk.sync
            def _(e):
                self.entry_waits(e, b)
                for g in range(NG2):
                    for q in range(8):
                        pg, pq = (g, q - 4) if q >= 4 else (g - 1, q + 4)
                        if pg >= 0:
                            e.wait_ge(scA.h, b["scA"] + act1_t(pg, pq))
                        e.dma_start(out=t["HTQ4"][:, (q % 4) * 1024:
                                                  (q % 4 + 1) * 1024],
                                    in_=bass.AP(t["hbuf"],
                                                q * 4 * 128 * R + g * T2,
                                                [[R, 128], [128 * R, 4], [1, T2]])
                                    ).then_inc(self.dsl[q % 4].h, 16)
                        self.dsl[q % 4].n += 16
                    xr = t["XR0"] if g % 2 == 0 else t["XR1"]
                    if g >= 2:
                        e.wait_ge(veV.h, b["veV"] + vO + 16 * (g - 1) + 8)
                    e.dma_start(out=xr[:, :],
                                in_=bass.AP(t[xsrc], g * T2,
                                            [[R, 128], [128 * R, ND], [1, T2]])
                                ).then_inc(self.dsb[g % 2].h, 16)
                    self.dsb[g % 2].n += 16

            @blk.scalar
            def _(e):
                def qh_t(g, q):
                    return (q + 1) if g == 0 else 16 * g - 8 + q + 1

                def acts(g):
                    for q in range(8):
                        e.wait_ge(self.dsl[q % 4].h,
                                  b["dsl%d" % (q % 4)]
                                  + 16 * ((g * 8 + q) // 4 + 1))
                        pg, pq = (g, q - 2) if q >= 2 else (g - 1, q + 6)
                        if pg >= 0:
                            e.wait_ge(veV.h, b["veV"] + vO + qh_t(pg, pq))
                        e.activation(t["ATQ"][:, (q % 2) * 1024:(q % 2 + 1) * 1024],
                                     t["HTQ4"][:, (q % 4) * 1024:(q % 4 + 1) * 1024],
                                     AF.Identity, scale=t["IS1C"][:, 0:1],
                                     bias=MAGIC).then_inc(scA.h, 1)

                def res_t(g, dt):
                    return (16 * (g + 1) + dt + 1) if g < NG2 - 1 \
                        else 16 * (NG2 - 1) + 8 + dt + 1

                def evacs(g):
                    for dt in range(ND):
                        gi = g * ND + dt
                        e.wait_ge(mmG.h, b["mmG"] + gi + 1)
                        if gi == 0:
                            e.wait_ge(veV.h, b["veV"] + vO)
                        if dt >= 4:
                            e.wait_ge(veV.h, b["veV"] + vO + res_t(g, dt - 4))
                        elif g >= 1:
                            e.wait_ge(veV.h, b["veV"] + vO + res_t(g - 1, dt + 4))
                        e.activation(t["YT%d" % (dt % 4)][:, :],
                                     PB[4 + dt % 4][:, 0:T2],
                                     AF.Identity, scale=t["S2SW"][:, dt:dt + 1],
                                     bias=t["B2Q"][:, dt:dt + 1]).then_inc(scA.h, 1)

                for g in range(NG2):
                    acts(g)
                    if g >= 1:
                        evacs(g - 1)
                evacs(NG2 - 1)
                scA.n = b["scA"] + 16 * NG2

            @blk.vector
            def _(e):
                self.derive_q(e, "SWV2", 8, "S2SW", "B2S", "B2Q", True)
                e.tensor_copy(t["TMPC"][:, 0:1], t["TMPC"][:, 0:1]
                              ).then_inc(veV.h, 1)

                def qhs(g):
                    qh = t["QH0"] if g % 2 == 0 else t["QH1"]
                    for q in range(8):
                        e.wait_ge(scA.h, b["scA"] + act1_t(g, q))
                        if g >= 2 and q == 0:
                            e.wait_ge(mmG.h, b["mmG"] + 8 * (g - 1))
                        e.tensor_scalar_add(qh[:, q * 1024:(q + 1) * 1024],
                                            t["ATQ"][:, (q % 2) * 1024:
                                                     (q % 2 + 1) * 1024],
                                            -MAGIC).then_inc(veV.h, 1)

                def residuals(g):
                    xr = t["XR0"] if g % 2 == 0 else t["XR1"]
                    xo = t["XO0"] if g % 2 == 0 else t["XO1"]
                    for dt in range(ND):
                        e.wait_ge(scA.h, b["scA"] + evac_t(g, dt))
                        if dt == 0:
                            e.wait_ge(self.dsb[g % 2].h,
                                      b["dsb%d" % (g % 2)] + 16 * (g // 2 + 1))
                            if g >= 2:
                                e.wait_ge(self.dstp[g % 2].h,
                                          b["dstp%d" % (g % 2)]
                                          + 16 * ((g - 2) // 2 + 1))
                        e.tensor_add(xo[:, dt * T2:(dt + 1) * T2],
                                     xr[:, dt * T2:(dt + 1) * T2],
                                     t["YT%d" % (dt % 4)][:, :]).then_inc(veV.h, 1)

                for g in range(NG2):
                    qhs(g)
                    if g >= 1:
                        residuals(g - 1)
                residuals(NG2 - 1)
                veV.n = b["veV"] + vO + 16 * NG2

            @blk.tensor
            def _(e):
                e.wait_ge(self.sqL.h, b["sqL"])
                for g in range(NG2):
                    qh = t["QH0"] if g % 2 == 0 else t["QH1"]
                    for dt in range(ND):
                        gi = g * ND + dt
                        e.wait_ge(veV.h, b["veV"] + vO
                                  + (8 if g == 0 else 16 * g))
                        if gi >= 4:
                            pg, pdt = divmod(gi - 4, ND)
                            e.wait_ge(scA.h, b["scA"] + evac_t(pg, pdt))
                        for k in range(NF):
                            mi = e.matmul(PB[4 + dt % 4][:, 0:T2],
                                          self.wbuf[k // 4][:, (k % 4) * 1024
                                                            + dt * 128:
                                                            (k % 4) * 1024
                                                            + (dt + 1) * 128],
                                          qh[:, k * T2:(k + 1) * T2],
                                          start=(k == 0), stop=(k == NF - 1))
                        mi.then_inc(mmG.h, 1)
                mmG.n = b["mmG"] + 8 * NG2

            @blk.gpsimd
            def _(e):
                for g in range(NG2):
                    xo = t["XO0"] if g % 2 == 0 else t["XO1"]
                    rt = 16 * (g + 1) + 8 if g < NG2 - 1 else 16 * NG2
                    e.wait_ge(veV.h, b["veV"] + vO + rt)
                    e.dma_start(out=bass.AP(t[dst], g * T2,
                                            [[R, 128], [128 * R, ND], [1, T2]]),
                                in_=xo[:, :]).then_inc(self.dstp[g % 2].h, 16)
                    self.dstp[g % 2].n += 16

    def load_wq(self, which):
        nc, t = self.nc, self.t
        sqL = self.sqL
        b = self.bases()
        with nc.Block() as blk:
            @blk.sync
            def _(e):
                self.entry_waits(e, b)
                n = 0
                if which == "w1q":
                    for c in range(ND):
                        e.dma_start(out=self.wbuf[c][:, :],
                                    in_=t["w1q"][c * 128:(c + 1) * 128, :]
                                    ).then_inc(sqL.h, 16)
                        n += 16
                elif which == "w2q":
                    for i in range(ND):
                        e.dma_start(out=self.wbuf[i][:, :],
                                    in_=bass.AP(t["w2q"], i * 4 * 128 * D,
                                                [[D, 128], [128 * D, 4], [1, D]])
                                    ).then_inc(sqL.h, 16)
                        n += 16
                else:
                    for c in range(ND):
                        e.dma_start(out=self.wbuf[c][:, 0:1024],
                                    in_=t["wwtb"][c * 128:(c + 1) * 128, :]
                                    ).then_inc(sqL.h, 16)
                        e.dma_start(out=self.wbuf[c][:, 1024:2048],
                                    in_=t["wrtb"][c * 128:(c + 1) * 128, :]
                                    ).then_inc(sqL.h, 16)
                        n += 32
                sqL.n = b["sqL"] + n

    def reset_max(self, name):
        nc, t = self.nc, self.t
        with nc.Block() as blk:
            @blk.vector
            def _(e):
                e.memset(t[name][:, :], NEGBIG)

    def copy_dram(self, src, dst):
        nc, t = self.nc, self.t
        sqL, gqS = self.sqL, self.gqS
        b = self.bases()
        with nc.Block() as blk:
            @blk.sync
            def _(e):
                self.entry_waits(e, b)
                for g in range(NTG):
                    if g >= 2:
                        e.wait_ge(self.dstp[g % 2].h,
                                  b["dstp%d" % (g % 2)] + 16 * ((g - 2) // 2 + 1))
                    xa = t["XA0"] if g % 2 == 0 else t["XA1"]
                    e.dma_start(out=xa[:, :],
                                in_=bass.AP(t[src], g * TG,
                                            [[R, 128], [128 * R, ND], [1, TG]])
                                ).then_inc(self.dsl[g % 2].h, 16)
                    self.dsl[g % 2].n += 16

            @blk.gpsimd
            def _(e):
                for g in range(NTG):
                    xa = t["XA0"] if g % 2 == 0 else t["XA1"]
                    e.wait_ge(self.dsl[g % 2].h,
                              b["dsl%d" % (g % 2)] + 16 * (g // 2 + 1))
                    e.dma_start(out=bass.AP(t[dst], g * TG,
                                            [[R, 128], [128 * R, ND], [1, TG]]),
                                in_=xa[:, :]).then_inc(self.dstp[g % 2].h, 16)
                    self.dstp[g % 2].n += 16

    def finish(self):
        nc = self.nc
        b = self.bases()
        sems = ([self.gqS, self.dln, self.drn] + self.dstp + self.hts
                + self.dtp + self.dsb)
        with nc.Block() as blk:
            @blk.gpsimd
            def _(e):
                for s_ in sems:
                    e.wait_ge(s_.h, s_.n)

    def ffn(self, src, dst, cc0, cc1):
        self.stats_ln(src, with_max=True, per_head=False, ccidx=cc0)
        self.mat1(ccidx=cc1)
        self.load_wq("w2q")
        self.mat2(src, dst)

    def build(self, upto=99):
        self.declare()
        ph = 0
        steps = [
            self.p0_scalars,
            self.p0_masks,
            lambda: self._quant_w2(),
            lambda: self._quant_w1(),
            lambda: self._weight_cast(),
            lambda: self.stats_ln("x0f", with_max=True, per_head=False,
                                  ccidx=1),
            lambda: self.mat1(ccidx=2),
            lambda: self.load_wq("w2q"),
            lambda: self.mat2("x0f", "x1f"),
            self.attn,
            lambda: self.load_wq("w1q"),
            lambda: self.stats_ln("x2f", with_max=True, per_head=False,
                                  ccidx=3),
            lambda: self.mat1(ccidx=4),
            lambda: self.load_wq("w2q"),
            lambda: self.mat2("x2f", "y"),
        ]
        self.const_init()
        for i, st in enumerate(steps):
            if i < upto:
                st()
        self.finish()



    def attn(self):
        self.stats_ln("x1f", with_max=False, per_head=True)
        self.load_wq("wwr")
        self.attn_all()

    def attn_all(self):
        nc, t = self.nc, self.t
        mmG, veV, scA, evS, stT = (self.mmG, self.veV, self.scA,
                                   self.evS, self.stT)
        PB = self.PB
        b = self.bases()

        vbl = [b["veV"]]
        for i in range(NTG):
            vbl.append(vbl[-1] + 61 + (1 if i % 4 == 0 else 0))
        vb = lambda tg: vbl[tg]
        eb = lambda tg: b["evS"] + 46 * tg
        mb = lambda tg: b["mmG"] + 28 * tg
        sb = lambda tg: b["stT"] + 65 * tg
        ab = lambda tg: b["scA"] + tg
        vstt0 = lambda tg: 21 + (1 if tg % 4 == 0 else 0)
        vstt = lambda tg, n, h: vb(tg) + vstt0(tg) + 4 * n + h + 1
        vres = lambda tg, dt: vb(tg) + vstt0(tg) + 32 + dt + 1
        sc_t = lambda tg, n, k: eb(tg) + 6 + 4 * n + k
        wr0 = lambda tg: eb(tg) + 38
        wkpm = lambda tg: t["WKPM"] if tg % 2 == 0 else t["WKPMB"]
        wkc_l = []
        c = 0
        for i in range(NTG):
            c += ND + (ND if i % 4 == 0 else 0)
            wkc_l.append(c)
        tp_T_end = lambda tg: 32 * (tg + 1) + 2 * tg
        tp_all = lambda tg: 34 * (tg + 1)

        def at_blk(h, n):
            a = (n // 2) % 2
            return t["ATB"][(n % 2) * 64:(n % 2) * 64 + 64,
                            (2 * h + n // 4) * 128 + a * 64:
                            (2 * h + n // 4) * 128 + a * 64 + 64]

        def grid(tn, h, n):
            a = (n // 2) % 2
            cb = (n % 2) + 2 * (2 * h + n // 4)
            return tn[a * 64:a * 64 + 64, cb * 64:cb * 64 + 64]

        def gpsum(which, h, n):
            a = (n // 2) % 2
            cb = (n % 2) + 2 * (2 * h + n // 4)
            bank = (0 if which == 0 else 2) + (0 if cb < 8 else 1)
            return PB[bank][a * 64:a * 64 + 64,
                            (cb % 8) * 64:(cb % 8) * 64 + 64], a

        with nc.Block() as blk:
            @blk.sync
            def _(e):
                self.entry_waits(e, b)

                def loads(tg):
                    for c_ in range(ND):
                        e.dma_start(out=self.rk_sl(c_),
                                    in_=bass.AP(t["rkbuf"],
                                                c_ * 128 * R + tg * TG,
                                                [[R, 128], [1, TG]])
                                    ).then_inc(self.dsl[2].h, 16)
                    for c_ in range(ND):
                        if tg % 4 == 0:
                            e.dma_start(out=self.wk_sl(c_)[:, 1:512],
                                        in_=bass.AP(t["rkbuf"],
                                                    c_ * 128 * R + tg * TG,
                                                    [[R, 128], [1, TG - 1]])
                                        ).then_inc(self.dsl[3].h, 16)
                            e.dma_start(out=self.wk_sl(c_)[:, 0:1],
                                        in_=t["ZCOL"][:, 0:1]
                                        ).then_inc(self.dsl[3].h, 16)
                        else:
                            e.dma_start(out=self.wk_sl(c_),
                                        in_=bass.AP(t["rkbuf"],
                                                    c_ * 128 * R + tg * TG - 1,
                                                    [[R, 128], [1, TG]])
                                        ).then_inc(self.dsl[3].h, 16)
                    for c_ in range(ND):
                        e.dma_start(out=self.ub_sl(c_),
                                    in_=bass.AP(t["ubbuf"],
                                                c_ * 128 * R + tg * TG,
                                                [[R, 128], [1, TG]])
                                    ).then_inc(self.dsb[0].h, 16)

                def transposes(tg):
                    e.wait_ge(self.dsl[3].h, b["dsl3"] + 16 * wkc_l[tg])
                    W = wkpm(tg)
                    for s in range(4):
                        for j in range(ND):
                            e.dma_start_transpose(
                                W[:, (s * 8 + j) * 128:(s * 8 + j + 1) * 128],
                                self.wk_sl(j)[:, s * 128:(s + 1) * 128]
                                ).then_inc(self.dtp[0].h, 16)

                loads(0)
                e.dma_start(out=t["AXA"][:, :],
                            in_=bass.AP(t["lnbuf"], 0,
                                        [[R, 128], [128 * R, ND], [1, TG]])
                            ).then_inc(self.dsl[0].h, 16)
                e.dma_start(out=t["AXB"][:, :],
                            in_=bass.AP(t["x1f"], 0,
                                        [[R, 128], [128 * R, ND], [1, TG]])
                            ).then_inc(self.dsl[1].h, 16)
                transposes(0)
                for tg in range(NTG):
                    e.wait_ge(scA.h, ab(tg) + 1)
                    e.dma_start(out=t["WKPMG2"][0:64, :],
                                in_=t["WKPMG"][64:128, :]
                                ).then_inc(self.dtp[0].h, 16)
                    e.dma_start(out=t["WKPMG2"][64:128, :],
                                in_=t["WKPMG"][0:64, :]
                                ).then_inc(self.dtp[0].h, 16)
                    if tg < NTG - 1:
                        e.wait_ge(mmG.h, mb(tg) + 10)
                        e.wait_ge(self.dtp[0].h, b["dtp0"] + 16 * tp_T_end(tg))
                        loads(tg + 1)
                        transposes(tg + 1)
                        e.wait_ge(veV.h, vres(tg, 7))
                        e.wait_ge(self.dstp[0].h, b["dstp0"] + 16 * (tg + 1))
                        e.dma_start(out=t["AXA"][:, :],
                                    in_=bass.AP(t["lnbuf"], (tg + 1) * TG,
                                                [[R, 128], [128 * R, ND],
                                                 [1, TG]])
                                    ).then_inc(self.dsl[0].h, 16)
                        e.dma_start(out=t["AXB"][:, :],
                                    in_=bass.AP(t["x1f"], (tg + 1) * TG,
                                                [[R, 128], [128 * R, ND],
                                                 [1, TG]])
                                    ).then_inc(self.dsl[1].h, 16)
                self.dsl[0].n += 16 * NTG
                self.dsl[1].n += 16 * NTG
                self.dsl[2].n += 16 * ND * NTG
                self.dsl[3].n += 16 * wkc_l[-1]
                self.dsb[0].n += 16 * ND * NTG
                self.dtp[0].n += 16 * tp_all(NTG - 1)

            @blk.gpsimd
            def _(e):
                for tg in range(NTG):
                    e.wait_ge(veV.h, vres(tg, 7))
                    e.dma_start(out=bass.AP(t["x2f"], tg * TG,
                                            [[R, 128], [128 * R, ND], [1, TG]]),
                                in_=t["AXB"][:, :]).then_inc(self.dstp[0].h, 16)
                self.dstp[0].n += 16 * NTG

            @blk.vector
            def _(e):
                for tg in range(NTG):
                    e.wait_ge(self.dsl[2].h, b["dsl2"] + 16 * ND * (tg + 1))
                    for h in range(H):
                        for j in range(2):
                            c_ = 2 * h + j
                            li = e.tensor_tensor(self.ga_sl(c_), self.rk_sl(c_),
                                                 self.GPAT[h][:, :], OP.mult)
                    li.then_inc(veV.h, 1)
                    for idx in range(8):
                        e.wait_ge(mmG.h, mb(tg) + idx + 1)
                        s, nh = divmod(idx, 2)
                        e.tensor_scalar_mul(
                            t["VPM"][:, s * 1024 + nh * 512:
                                     s * 1024 + nh * 512 + 512],
                            PB[2 + idx % 2][:, :], -1.0).then_inc(veV.h, 1)
                    e.wait_ge(mmG.h, mb(tg) + 9)
                    e.tensor_tensor(t["PSB"][:, 0:512], PB[0][:, :],
                                    t["NEGLS"][:, 0:512], OP.mult)
                    e.tensor_tensor(t["PSB"][:, 512:1024], PB[1][:, :],
                                    t["NEGLS"][:, 512:1024], OP.mult)
                    e.tensor_tensor(t["PTSB"][:, 0:512], PB[0][:, :],
                                    t["NEGLST"][:, 0:512], OP.mult)
                    li = e.tensor_tensor(t["PTSB"][:, 512:1024], PB[1][:, :],
                                         t["NEGLST"][:, 512:1024], OP.mult)
                    li.then_inc(veV.h, 1)
                    e.wait_ge(mmG.h, mb(tg) + 10)
                    e.tensor_tensor(t["INTRAT"][:, 0:512], PB[2][:, :],
                                    t["ALPHALMT"][:, 0:512], OP.mult)
                    e.tensor_tensor(t["INTRAT"][:, 512:1024], PB[3][:, :],
                                    t["ALPHALMT"][:, 512:1024], OP.mult
                                    ).then_inc(veV.h, 1)
                    li = e.tensor_tensor(t["XSB"][:, :], t["PSB"][:, :],
                                         t["EYEB"][:, :], OP.add)
                    li.then_inc(veV.h, 1)
                    for rnd in range(5):
                        if rnd < 4:
                            e.wait_ge(mmG.h, mb(tg) + 11 + 2 * rnd)
                            e.tensor_copy(t["QSB"][:, 0:512], PB[4][:, :])
                            li = e.tensor_copy(t["QTSB"][:, 0:512], PB[6][:, :])
                            li.then_inc(veV.h, 1)
                        e.wait_ge(mmG.h, mb(tg) + 12 + 2 * rnd)
                        e.tensor_copy(t["XSB"][:, 0:512], PB[2][:, :])
                        li = e.tensor_copy(t["XSB"][:, 512:1024], PB[3][:, :])
                        li.then_inc(veV.h, 1)
                    if tg % 4 == 0:
                        li = e.memset(t["SBFT"][:, :], 0.0)
                        li.then_inc(veV.h, 1)
                    for n in range(8):
                        for h in range(H):
                            e.wait_ge(stT.h, sb(tg) + 1 + 8 * n + 5 + h)
                            li = e.scalar_tensor_tensor(
                                t["SBFT"][:, h * 512:(h + 1) * 512],
                                t["SBFT"][:, h * 512:(h + 1) * 512],
                                t["GC128"][:, h:h + 1],
                                PB[4 + h][:, 0:512], OP.mult, OP.add)
                            li.then_inc(veV.h, 1)
                    e.wait_ge(self.dsl[0].h, b["dsl0"] + 16 * (tg + 1))
                    e.wait_ge(self.dsl[1].h, b["dsl1"] + 16 * (tg + 1))
                    for dt in range(ND):
                        e.wait_ge(evS.h, wr0(tg) + dt + 1)
                        e.tensor_add(t["AXB"][:, dt * TG:(dt + 1) * TG],
                                     t["AXB"][:, dt * TG:(dt + 1) * TG],
                                     t["AXA"][:, dt * TG:(dt + 1) * TG])
                        li = e.tensor_add(t["AXB"][:, dt * TG:(dt + 1) * TG],
                                          t["AXB"][:, dt * TG:(dt + 1) * TG],
                                          t["ATT"][:, :])
                        li.then_inc(veV.h, 1)
                veV.n = vb(NTG)

            @blk.scalar
            def _(e):
                for tg in range(NTG):
                    e.wait_ge(self.dtp[0].h, b["dtp0"] + 16 * tp_T_end(tg))
                    W = wkpm(tg)
                    for s in range(4):
                        for h in range(H):
                            base = (s * 8 + 2 * h) * 128
                            li = e.activation(t["WKPMG"][:, base:base + 256],
                                              W[:, base:base + 256],
                                              AF.Identity,
                                              scale=t["GT128"][:, h:h + 1])
                    li.then_inc(scA.h, 1)
                    for rnd in range(5):
                        e.wait_ge(mmG.h, mb(tg) + 11 + 2 * rnd)
                        if rnd < 4:
                            e.activation(t["QSB"][:, 512:1024], PB[5][:, :],
                                         AF.Identity)
                        else:
                            e.activation(t["QTSB"][:, 0:512], PB[6][:, :],
                                         AF.Identity)
                        li = e.activation(t["QTSB"][:, 512:1024], PB[7][:, :],
                                          AF.Identity)
                        li.then_inc(evS.h, 1)
                    e.wait_ge(stT.h, sb(tg) + 1)
                    for m in range(8):
                        li = e.activation(t["ATB"][:, m * 128:(m + 1) * 128],
                                          PB[m // 4][:, (m % 4) * 128:
                                                     (m % 4 + 1) * 128],
                                          AF.Identity)
                    li.then_inc(evS.h, 1)
                    for n in range(8):
                        a = (n // 2) % 2
                        e.wait_ge(stT.h, sb(tg) + 1 + 8 * n + 1)
                        e.activation(t["WKC"][:, :], PB[0][:, :], AF.Identity
                                     ).then_inc(evS.h, 1)
                        e.wait_ge(stT.h, sb(tg) + 1 + 8 * n + 2)
                        e.activation(t["VNB"][a * 64:a * 64 + 64, 0:512],
                                     PB[1][a * 64:a * 64 + 64, :], AF.Identity
                                     ).then_inc(evS.h, 1)
                        e.wait_ge(stT.h, sb(tg) + 1 + 8 * n + 3)
                        e.activation(t["VNB"][a * 64:a * 64 + 64, 512:1024],
                                     PB[2][a * 64:a * 64 + 64, :], AF.Identity
                                     ).then_inc(evS.h, 1)
                        e.wait_ge(stT.h, sb(tg) + 1 + 8 * n + 4)
                        for b2 in range(8):
                            li = e.activation(
                                t["OSB"][:, b2 * 512 + n * 64:
                                         b2 * 512 + n * 64 + 64],
                                PB[3][:, b2 * 64:(b2 + 1) * 64], AF.Identity)
                        li.then_inc(evS.h, 1)
                    for dt in range(ND):
                        e.wait_ge(mmG.h, mb(tg) + 20 + dt + 1)
                        if dt > 0:
                            e.wait_ge(veV.h, vres(tg, dt - 1))
                        e.activation(t["ATT"][:, :], PB[4 + dt % 4][:, :],
                                     AF.Identity).then_inc(evS.h, 1)
                scA.n = ab(NTG)
                evS.n = eb(NTG)

            @blk.tensor
            def _(e):
                e.wait_ge(self.sqL.h, b["sqL"])
                for tg in range(NTG):
                    W = wkpm(tg)
                    e.wait_ge(self.dsb[0].h, b["dsb0"] + 16 * ND * (tg + 1))
                    if tg > 0:
                        e.wait_ge(evS.h, eb(tg) - 8)
                    for idx in range(8):
                        s, nh = divmod(idx, 2)
                        psum = PB[2 + idx % 2]
                        if idx >= 2:
                            e.wait_ge(veV.h, vb(tg) + idx)
                        for k in range(ND):
                            mi = e.matmul(psum[:, :],
                                          self.ub_sl(k)[:, s * 128:(s + 1) * 128],
                                          self.wbuf[k][:, nh * 512:(nh + 1) * 512],
                                          start=(k == 0), stop=(k == ND - 1))
                        mi.then_inc(mmG.h, 1)
                    e.wait_ge(self.dsl[3].h, b["dsl3"] + 16 * wkc_l[tg])
                    e.wait_ge(self.dsl[2].h, b["dsl2"] + 16 * ND * (tg + 1))
                    for h in range(H):
                        for n in range(8):
                            blkp, a = gpsum(0, h, n)
                            for jj in range(2):
                                c_ = 2 * h + jj
                                mi = e.matmul(blkp,
                                              self.wk_sl(c_)[:, n * 64:(n + 1) * 64],
                                              self.wk_sl(c_)[:, n * 64:(n + 1) * 64],
                                              start=(jj == 0), stop=(jj == 1),
                                              tile_position=(0, a * 64))
                    mi.then_inc(mmG.h, 1)
                    e.wait_ge(veV.h, vb(tg) + 9)
                    for h in range(H):
                        for n in range(8):
                            blkp, a = gpsum(1, h, n)
                            for jj in range(2):
                                c_ = 2 * h + jj
                                mi = e.matmul(blkp,
                                              self.wk_sl(c_)[:, n * 64:(n + 1) * 64],
                                              self.rk_sl(c_)[:, n * 64:(n + 1) * 64],
                                              start=(jj == 0), stop=(jj == 1),
                                              tile_position=(0, a * 64))
                    mi.then_inc(mmG.h, 1)
                    e.wait_ge(veV.h, vb(tg) + 12)
                    if tg > 0:
                        e.wait_ge(veV.h, vb(tg) - 8)
                    for h in range(H):
                        for n in range(8):
                            a = (n // 2) % 2
                            cb = (n % 2) + 2 * (2 * h + n // 4)
                            px = PB[2 if cb < 8 else 3][a * 64:a * 64 + 64,
                                                        (cb % 8) * 64:
                                                        (cb % 8) * 64 + 64]
                            mi = e.matmul(px,
                                          t["EYE128"][a * 64:a * 64 + 64, 0:64],
                                          grid(t["XSB"], h, n), start=True,
                                          stop=False,
                                          tile_position=(a * 64, a * 64))
                    qsrc, qtsrc = "PSB", "PTSB"
                    e.wait_ge(evS.h, eb(tg))
                    for rnd in range(5):
                        if rnd >= 1:
                            e.wait_ge(veV.h, vb(tg) + 11 + 2 * rnd)
                            e.wait_ge(evS.h, eb(tg) + rnd)
                        for h in range(H):
                            for n in range(8):
                                a = (n // 2) % 2
                                cb = (n % 2) + 2 * (2 * h + n // 4)
                                if rnd < 4:
                                    pq = PB[4 if cb < 8 else 5][
                                        a * 64:a * 64 + 64,
                                        (cb % 8) * 64:(cb % 8) * 64 + 64]
                                    e.matmul(pq, grid(t[qtsrc], h, n),
                                             grid(t[qsrc], h, n),
                                             start=True, stop=True,
                                             tile_position=(a * 64, a * 64))
                                pqt = PB[6 if cb < 8 else 7][
                                    a * 64:a * 64 + 64,
                                    (cb % 8) * 64:(cb % 8) * 64 + 64]
                                mi = e.matmul(pqt, grid(t[qsrc], h, n),
                                              grid(t[qtsrc], h, n),
                                              start=True, stop=True,
                                              tile_position=(a * 64, a * 64))
                        mi.then_inc(mmG.h, 1)
                        qsrc, qtsrc = "QSB", "QTSB"
                        e.wait_ge(evS.h, eb(tg) + rnd + 1)
                        if rnd < 4:
                            e.wait_ge(veV.h, vb(tg) + 13 + 2 * rnd)
                        if rnd >= 1:
                            e.wait_ge(veV.h, vb(tg) + 12 + 2 * rnd)
                        for h in range(H):
                            for n in range(8):
                                a = (n // 2) % 2
                                cb = (n % 2) + 2 * (2 * h + n // 4)
                                px = PB[2 if cb < 8 else 3][
                                    a * 64:a * 64 + 64,
                                    (cb % 8) * 64:(cb % 8) * 64 + 64]
                                mi = e.matmul(px, grid(t["QTSB"], h, n),
                                              grid(t["XSB"], h, n),
                                              start=False, stop=(rnd == 4),
                                              tile_position=(a * 64, a * 64))
                        mi.then_inc(mmG.h, 1)
                    e.wait_ge(veV.h, vb(tg) + 21)
                    for m in range(8):
                        mi = e.matmul(PB[m // 4][:, (m % 4) * 128:
                                                 (m % 4 + 1) * 128],
                                      t["XSB"][:, m * 128:(m + 1) * 128],
                                      t["EYEF"][:, :], start=True, stop=True)
                    mi.then_inc(stT.h, 1)
                    e.wait_ge(self.dtp[0].h, b["dtp0"] + 16 * tp_all(tg))
                    for n in range(8):
                        a = (n // 2) % 2
                        p0 = (n % 2) * 64
                        sbase = (n // 2) * 8
                        if n > 0:
                            e.wait_ge(evS.h, sc_t(tg, n - 1, 1))
                        else:
                            e.wait_ge(evS.h, eb(tg) + 6)
                        for h in range(H):
                            for cs in range(2):
                                base = (sbase + 2 * h) * 128
                                mi = e.matmul(
                                    PB[0][:, (2 * h + cs) * 64:
                                          (2 * h + cs) * 64 + 64],
                                    W[p0:p0 + 64, base + cs * 128:
                                      base + (cs + 1) * 128],
                                    at_blk(h, n), start=True, stop=True,
                                    tile_position=(p0, 0))
                        mi.then_inc(stT.h, 1)
                        e.wait_ge(evS.h, sc_t(tg, n, 1))
                        if n > 0:
                            e.wait_ge(evS.h, sc_t(tg, n - 1, 3))
                        else:
                            e.wait_ge(veV.h, vb(tg) + vstt0(tg))
                        for h in range(H):
                            if n > 0:
                                e.wait_ge(veV.h, vstt(tg, n - 1, h))
                            psum = PB[1] if h < 2 else PB[2]
                            col = (h % 2) * 256
                            for cs in range(2):
                                e.matmul(psum[a * 64:a * 64 + 64, col:col + 256],
                                         t["WKC"][:, (2 * h + cs) * 64:
                                                  (2 * h + cs) * 64 + 64],
                                         t["SBFT"][:, (2 * h + cs) * 256:
                                                   (2 * h + cs) * 256 + 256],
                                         start=(cs == 0), stop=False,
                                         tile_position=(0, a * 64))
                            mi = e.matmul(psum[a * 64:a * 64 + 64, col:col + 256],
                                          at_blk(h, n),
                                          t["VPM"][p0:p0 + 64,
                                                   (n // 2) * 1024 + h * 256:
                                                   (n // 2) * 1024 + h * 256 + 256],
                                          start=False, stop=True,
                                          tile_position=(p0, a * 64))
                            if h == 1 or h == 3:
                                mi.then_inc(stT.h, 1)
                        e.wait_ge(evS.h, sc_t(tg, n, 3))
                        if n > 0:
                            e.wait_ge(evS.h, sc_t(tg, n - 1, 4))
                        for h in range(H):
                            for cs in range(2):
                                ob = PB[3][:, (2 * h + cs) * 64:
                                           (2 * h + cs) * 64 + 64]
                                for es in range(2):
                                    c_ = 2 * h + es
                                    e.matmul(ob,
                                             t["SBFT"][:, c_ * 256 + cs * 128:
                                                       c_ * 256 + (cs + 1) * 128],
                                             self.ga_sl(c_)[:, n * 64:(n + 1) * 64],
                                             start=(es == 0), stop=False)
                                mi = e.matmul(ob,
                                              t["VNB"][a * 64:a * 64 + 64,
                                                       h * 256 + cs * 128:
                                                       h * 256 + (cs + 1) * 128],
                                              grid(t["INTRAT"], h, n),
                                              start=False, stop=True,
                                              tile_position=(a * 64, 0))
                        mi.then_inc(stT.h, 1)
                        use_orig = (n % 2) == a
                        G = t["WKPMG"] if use_orig else t["WKPMG2"]
                        for h in range(H):
                            base = (sbase + 2 * h) * 128
                            for es in range(2):
                                mi = e.matmul(
                                    PB[4 + h][:, es * 256:(es + 1) * 256],
                                    G[a * 64:a * 64 + 64,
                                      base + es * 128:base + (es + 1) * 128],
                                    t["VNB"][a * 64:a * 64 + 64,
                                             h * 256:h * 256 + 256],
                                    start=True, stop=True,
                                    tile_position=(a * 64, 0))
                            mi.then_inc(stT.h, 1)
                    e.wait_ge(evS.h, wr0(tg))
                    for dt in range(ND):
                        if dt < 4:
                            e.wait_ge(veV.h, vstt(tg, 7, dt))
                        else:
                            e.wait_ge(evS.h, wr0(tg) + (dt - 4) + 1)
                        for k in range(ND):
                            mi = e.matmul(PB[4 + dt % 4][:, :],
                                          self.wbuf[k][:, 1024 + dt * 128:
                                                       1024 + (dt + 1) * 128],
                                          t["OSB"][:, k * TG:(k + 1) * TG],
                                          start=(k == 0), stop=(k == ND - 1))
                        mi.then_inc(mmG.h, 1)
                stT.n = sb(NTG)
                mmG.n = mb(NTG)

def make_constants():
    i = np.arange(C)
    diffs = np.maximum(i[:, None] - i[None, :], 0).astype(np.float32)
    strict = (i[:, None] > i[None, :]).astype(np.float32)
    eye = np.eye(C, dtype=np.float32)
    tile = lambda m: np.tile(m, (2, 16)).astype(np.float32)
    return {
        "cdiffsg": tile(diffs),
        "cnstrictg": tile(-strict),
        "cdiffstg": tile(diffs.T),
        "cnstricttg": tile(-strict.T),
        "cstricttg": tile(strict.T),
        "ceyeb": tile(eye),
        "ceye128": np.tile(eye, (2, 1)).astype(np.float32),
    "ceyef": np.eye(128, dtype=np.float32),
        "ciota": i[None, :].astype(np.float32),
        "criota": i[::-1][None, :].astype(np.float32).copy(),
    }


def shard_inputs(L0_attn_out, L0_attn_Ww, L0_attn_Wr, L0_attn_decay,
                 L0_attn_log_alpha, L0_ffn_w1, L0_ffn_b1, L0_ffn_w2, L0_ffn_b2):
    cn = make_constants()
    common = {
        "w1": np.ascontiguousarray(np.asarray(L0_ffn_w1, np.float32)),
        "w2": np.ascontiguousarray(np.asarray(L0_ffn_w2, np.float32)),
        "wwt": np.ascontiguousarray(np.asarray(L0_attn_Ww, np.float32).T),
        "wrt": np.ascontiguousarray(np.asarray(L0_attn_Wr, np.float32).T),
        "b1r": np.ascontiguousarray(np.asarray(L0_ffn_b1, np.float32).reshape(32, 128).T),
        "b2r": np.ascontiguousarray(np.asarray(L0_ffn_b2, np.float32).reshape(8, 128).T),
        "decay": np.asarray(L0_attn_decay, np.float32).reshape(1, H),
        "logalpha": np.asarray(L0_attn_log_alpha, np.float32).reshape(1, H),
    }
    common.update(cn)
    x = np.asarray(L0_attn_out, np.float32)
    in_maps = []
    for c in range(8):
        m = dict(common)
        m["x0f"] = np.ascontiguousarray(x[2 * c:2 * c + 2].reshape(R, D).T)
        in_maps.append(m)
    return in_maps


def build_program(dump=(), upto=99, races=True):
    nc = bass.Bass("TRN2", target_bir_lowering=False,
                   detect_race_conditions=races)
    b = Builder(nc, dump=dump)
    b.build(upto=upto)
    return nc


def kernel(**inputs):
    in_maps = shard_inputs(**inputs)
    nc = build_program(DUMP)
    res = run_bass_kernel_spmd(nc, in_maps, core_ids=list(range(8)))
    out = np.empty((16, 2048, 1024), np.float32)
    for c in range(8):
        out[2 * c:2 * c + 2] = res.results[c]["y"].T.reshape(2, 2048, 1024)
    return out
